# version 42
# baseline (speedup 1.0000x reference)
import numpy as np
import ml_dtypes
import concourse.bass as bass
import concourse.mybir as mybir
from concourse.bass_utils import run_bass_kernel_spmd
from contextlib import ExitStack

F32 = mybir.dt.float32
BF16 = mybir.dt.bfloat16
AF = mybir.ActivationFunctionType
OP = mybir.AluOpType
AX = mybir.AxisListType
bfd = ml_dtypes.bfloat16

D = 2048
S = 2048
NDC = 16
NTT = 4
TT = 512
H = 8192
EPS_LN = 1e-5
EPS_RMS = float(np.finfo(np.float32).eps)


class Emitter:

    def __init__(self, which, handle, sems, base):
        self.which = which
        self.h = handle
        self.sems = sems
        self.n = dict(base)

    def dma(self, q, out=None, in_=None, **kw):
        eng = {"g": "gpsimd", "s": "sync", "a": "scalar", "v": "gpsimd"}[q]
        self.n[q] += 1
        if self.which == eng:
            self.h.dma_start(out=out, in_=in_, **kw).then_inc(self.sems[q], 16)
        return self.n[q]

    def tdma(self, q, out, in_):
        eng = {"g": "gpsimd", "s": "sync", "a": "scalar", "v": "gpsimd"}[q]
        self.n[q] += 1
        if self.which == eng:
            self.h.dma_start_transpose(out=out, in_=in_).then_inc(self.sems[q], 16)
        return self.n[q]

    def op(self, eng, fn, inc=False):
        if self.which == eng:
            inst = fn(self.h)
            if inc:
                inst.then_inc(self.sems[eng], 1)
        if inc:
            self.n[eng] += 1
            return self.n[eng]
        return None

    def wait(self, eng, sem_name, ticket):
        if ticket is None or (isinstance(ticket, int) and ticket <= 0):
            return
        if self.which == eng:
            mult = 16 if sem_name in ("g", "s", "a", "v") else 1
            self.h.wait_ge(self.sems[sem_name], mult * ticket)


def run_phase(nc, sems, base, phase_fn):
    finals = {}
    with nc.Block() as block:
        def mk(which):
            def body(handle):
                E = Emitter(which, handle, sems, base)
                phase_fn(E)
                finals[which] = dict(E.n)
            return body
        block.tensor(mk("tensor"))
        block.vector(mk("vector"))
        block.scalar(mk("scalar"))
        block.gpsimd(mk("gpsimd"))
        block.sync(mk("sync"))
    f0 = finals["tensor"]
    for k, v in finals.items():
        assert v == f0, f"ticket divergence {k}: {v} vs {f0}"
    return f0


def build_program(max_phase=99):
    nc = bass.Bass(trn_type="TRN2", target_bir_lowering=False)

    x_in = nc.dram_tensor("x", [S, D], F32, kind="ExternalInput").ap()
    wq = nc.dram_tensor("wq", [D, D], F32, kind="ExternalInput").ap()
    wk = nc.dram_tensor("wk", [512, D], F32, kind="ExternalInput").ap()
    wv = nc.dram_tensor("wv", [512, D], F32, kind="ExternalInput").ap()
    wo = nc.dram_tensor("wo", [D, D], F32, kind="ExternalInput").ap()
    wgu = nc.dram_tensor("wgu", [2 * H, D], F32, kind="ExternalInput").ap()
    wp = nc.dram_tensor("wp", [D, H], F32, kind="ExternalInput").ap()
    c2t = nc.dram_tensor("c2t", [64, S], F32, kind="ExternalInput").ap()
    s2t = nc.dram_tensor("s2t", [64, S], F32, kind="ExternalInput").ap()
    tri_in = nc.dram_tensor("tri", [128, 128], BF16, kind="ExternalInput").ap()
    onesf_in = nc.dram_tensor("onesf", [1, 128], F32, kind="ExternalInput").ap()
    halvesf_in = nc.dram_tensor("halvesf", [1, 128], F32, kind="ExternalInput").ap()
    onesbf_in = nc.dram_tensor("onesbf", [128, 1], BF16, kind="ExternalInput").ap()
    idf_in = nc.dram_tensor("idf", [128, 128], F32, kind="ExternalInput").ap()
    epsc_in = nc.dram_tensor("epsc", [1, 2], F32, kind="ExternalInput").ap()
    out = nc.dram_tensor("out", [S, D], F32, kind="ExternalOutput").ap()

    x0f = nc.dram_tensor("x0f", [NDC, NTT, 128, TT], F32, kind="Internal").ap()
    x0fb = nc.dram_tensor("x0fb", [NDC, NTT, 128, TT], BF16, kind="Internal").ap()
    x1f = nc.dram_tensor("x1f", [NDC, NTT, 128, TT], F32, kind="Internal").ap()
    x2f = nc.dram_tensor("x2f", [NDC, NTT, 128, TT], F32, kind="Internal").ap()
    hbuf = nc.dram_tensor("hbuf", [H, S], BF16, kind="Internal").ap()
    wgu_q = nc.dram_tensor("wgu_q", [2 * H, D], BF16, kind="Internal").ap()
    wp_q = nc.dram_tensor("wp_q", [D, H], BF16, kind="Internal").ap()
    wq_s = nc.dram_tensor("wq_s", [D, D], BF16, kind="Internal").ap()
    wk_s = nc.dram_tensor("wk_s", [512, D], BF16, kind="Internal").ap()
    wv_s = nc.dram_tensor("wv_s", [512, D], BF16, kind="Internal").ap()
    wo_s = nc.dram_tensor("wo_s", [D, D], BF16, kind="Internal").ap()

    gctx = ExitStack()
    gsb = lambda name, shape, dt: gctx.enter_context(nc.sbuf_tensor(name, shape, dt))
    idf = gsb("idf_s", [128, 128], F32)
    onesf = gsb("onesf_s", [1, 128], F32)
    halvesf = gsb("halvesf_s", [1, 128], F32)
    onesbf = gsb("onesbf_s", [128, 1], BF16)
    tri = gsb("tri_s", [128, 128], BF16)
    epsc = gsb("epsc_s", [1, 2], F32)

    sems = {}
    for name in ("g", "s", "a", "v", "tensor", "vector", "scalar", "gpsimd"):
        sems[name] = gctx.enter_context(nc.semaphore(f"sem_{name}"))
    base = [{k: 0 for k in ("g", "s", "a", "v", "tensor", "vector", "scalar", "gpsimd")}]

    class QuantPump:
        def __init__(self, E, Qsets):
            self.E = E
            self.Q = Qsets
            self.done_t = {}
            self.wr = [None, None]
            self.loaded = {}
            self.u = 0
            self.NU = 192

        def _params(self, u):
            if u < 128:
                half, idx = u % 2, u // 2
                r0 = (H if half else 0) + 128 * idx
                return (wgu[r0:r0 + 128, :], wgu_q[r0:r0 + 128, :])
            v = u - 128
            q, slab = v // 16, v % 16
            return (wp[128 * slab:128 * slab + 128, 2048 * q:2048 * q + 2048],
                    wp_q[128 * slab:128 * slab + 128, 2048 * q:2048 * q + 2048])

        def _emit_load(self, u):
            if u >= self.NU or u in self.loaded:
                return
            E = self.E
            s = u % 2
            qb = self.Q[s][0]
            src, _ = self._params(u)
            E.wait("gpsimd", "a", self.wr[s])
            self.loaded[u] = E.dma("v", out=qb[:], in_=src)

        def _emit_unit(self, u):
            E = self.E
            s = u % 2
            qb, ta, tb, qsums, qsbf, qshalf, qnshalf = self.Q[s]
            src, dst = self._params(u)
            self._emit_load(u)
            t_in = self.loaded[u]
            self._emit_load(u + 1)
            E.wait("vector", "v", t_in)
            t_red = E.op("vector", lambda v: v.tensor_reduce(
                out=qsums[:, 0:32], in_=qb[:].rearrange("p (g k) -> p g k", k=64),
                axis=AX.X, op=OP.add, apply_absolute_value=True), inc=True)
            E.wait("scalar", "vector", t_red)
            t_mul = E.op("scalar", lambda sc: sc.mul(qsbf[:, 0:32], qsums[:, 0:32], 1.0 / 64.0), inc=True)
            E.wait("vector", "scalar", t_mul)
            t = E.op("vector", lambda v: v.tensor_scalar_max(qsbf[:, 0:32], qsbf[:, 0:32], 1e-8), inc=True)
            E.wait("vector", "vector", t)
            E.op("vector", lambda v: v.tensor_scalar_mul(qshalf[:, 0:32], qsbf[:, 0:32], 0.5), inc=True)
            t = E.op("vector", lambda v: v.tensor_scalar_mul(qnshalf[:, 0:32], qsbf[:, 0:32], -0.5), inc=True)
            E.wait("vector", "vector", t)
            E.op("vector", lambda v: v.tensor_tensor(
                out=ta[:].rearrange("p (g k) -> p g k", k=64),
                in0=qb[:].rearrange("p (g k) -> p g k", k=64),
                in1=qshalf[:, 0:32].unsqueeze(2).broadcast_to([128, 32, 64]), op=OP.is_gt), inc=True)
            t = E.op("vector", lambda v: v.tensor_tensor(
                out=tb[:].rearrange("p (g k) -> p g k", k=64),
                in0=qb[:].rearrange("p (g k) -> p g k", k=64),
                in1=qnshalf[:, 0:32].unsqueeze(2).broadcast_to([128, 32, 64]), op=OP.is_lt), inc=True)
            E.wait("vector", "vector", t)
            t = E.op("vector", lambda v: v.tensor_sub(qb[:], ta[:], tb[:]), inc=True)
            E.wait("vector", "vector", t)
            t_q = E.op("vector", lambda v: v.tensor_tensor(
                out=ta[:].rearrange("p (g k) -> p g k", k=64),
                in0=qb[:].rearrange("p (g k) -> p g k", k=64),
                in1=qsbf[:, 0:32].unsqueeze(2).broadcast_to([128, 32, 64]), op=OP.mult), inc=True)
            E.wait("scalar", "vector", t_q)
            t_wr = E.dma("a", out=dst, in_=ta[:])
            self.wr[s] = t_wr
            self.done_t[u] = t_wr

        def units(self, n):
            for _ in range(n):
                if self.u >= self.NU:
                    return
                self._emit_unit(self.u)
                self.u += 1

        def drain(self):
            self.units(self.NU - self.u)

    wprep = {}

    def emit_staging(E):
        wqv = wq.rearrange("(j2 head t pair par) d -> j2 t par head pair d",
                           j2=4, head=2, t=4, pair=32, par=2)
        for s_ in range(16):
            tt_ = None
            for blkk in range(4):
                par_, head_ = blkk // 2, blkk % 2
                tt_ = E.dma("g", out=wq_s[128 * s_ + 32 * blkk:128 * s_ + 32 * blkk + 32, :],
                            in_=wqv[s_ // 4, s_ % 4, par_, head_])
            wprep[("q", s_)] = tt_
        wkv = wk.rearrange("(j2 head pair par) d -> j2 par head pair d",
                           j2=4, head=2, pair=32, par=2)
        for s_ in range(4):
            tt_ = None
            for blkk in range(4):
                par_, head_ = blkk // 2, blkk % 2
                tt_ = E.dma("g", out=wk_s[128 * s_ + 32 * blkk:128 * s_ + 32 * blkk + 32, :],
                            in_=wkv[s_, par_, head_])
            wprep[("k", s_)] = tt_
        wprep[("v", 0)] = E.dma("g", out=wv_s, in_=wv)
        wprep[("o", 0)] = E.dma("g", out=wo_s, in_=wo)

    def emit_prologue(E, x_src, z_dst, B, nbuf=2, ph0=None, pump=None, lq="g"):
        (xcache, xsqb, row_e2, row_msq, row_var, row_ln, zt_f32,
         stats_m, stats_r, ps_st0, ps_st1, ps_bcm, ps_bcr) = B
        le = {"g": "gpsimd", "s": "sync"}[lq]
        z_done = {}
        zop = {}
        sq_of = {}
        srow_done = {}
        bc_done = {}
        for tt in range(NTT):
            xc_off = (tt % nbuf) * (NDC * TT)
            st1_of_dc = {}
            for dc in range(NDC):
                if pump is not None and dc % 2 == 1:
                    pump.units(1)
                E.wait(le, "vector", zop.get((tt - nbuf, dc)))
                E.wait(le, "scalar", sq_of.get((tt - nbuf, dc)))
                if ph0 is not None:
                    E.wait(le, "s", ph0[(dc, tt)])
                t_ld = E.dma(lq, out=xcache[:, xc_off + TT * dc: xc_off + TT * dc + TT],
                             in_=x_src[dc, tt])
                E.wait("scalar", lq, t_ld)
                E.wait("scalar", "tensor", st1_of_dc.get(dc - 2))
                t_sq = E.op("scalar", lambda sc, dc=dc: sc.activation(
                    out=xsqb[:, TT * (dc % 2):TT * (dc % 2) + TT],
                    in_=xcache[:, xc_off + TT * dc: xc_off + TT * dc + TT],
                    func=AF.Square), inc=True)
                sq_of[(tt, dc)] = t_sq
                E.wait("tensor", lq, t_ld)
                E.wait("tensor", "scalar", t_sq)
                if dc == 0:
                    E.wait("tensor", "scalar", srow_done.get(tt - 1))
                E.op("tensor", lambda t, dc=dc: t.matmul(
                    ps_st0[:], onesbf[:],
                    xcache[:, xc_off + TT * dc: xc_off + TT * dc + TT],
                    start=(dc == 0), stop=(dc == NDC - 1)))
                t_mm = E.op("tensor", lambda t, dc=dc: t.matmul(
                    ps_st1[:], onesbf[:],
                    xsqb[:, TT * (dc % 2):TT * (dc % 2) + TT],
                    start=(dc == 0), stop=(dc == NDC - 1)), inc=True)
                st1_of_dc[dc] = t_mm
            E.wait("scalar", "tensor", t_mm)
            E.wait("scalar", "tensor", bc_done.get(tt - 1))
            E.op("scalar", lambda sc, tt=tt: sc.mul(stats_m[:, 0:TT], ps_st0[:], 1.0 / D))
            t_e2 = E.op("scalar", lambda sc: sc.mul(row_e2[:], ps_st1[:], 1.0 / D), inc=True)
            t_msq = E.op("scalar", lambda sc, tt=tt: sc.activation(
                out=row_msq[:], in_=stats_m[:, 0:TT], func=AF.Square), inc=True)
            srow_done[tt] = t_msq
            E.wait("vector", "scalar", t_msq)
            t_var = E.op("vector", lambda v: v.tensor_sub(row_var[:], row_e2[:], row_msq[:]), inc=True)
            E.wait("scalar", "vector", t_var)
            t_ln = E.op("scalar", lambda sc: sc.activation(
                out=row_ln[:], in_=row_var[:], func=AF.Ln, bias=epsc[0:1, 0:1]), inc=True)
            E.wait("scalar", "scalar", t_ln)
            t_rs = E.op("scalar", lambda sc, tt=tt: sc.activation(
                out=stats_r[:, 0:TT], in_=row_ln[:], func=AF.Exp, scale=-0.5), inc=True)
            E.wait("tensor", "scalar", t_rs)
            E.wait("tensor", "vector", z_done.get(tt - 1))
            E.op("tensor", lambda t, tt=tt: t.matmul(
                ps_bcm[:], onesf[:], stats_m[:, 0:TT], start=True, stop=True))
            t_bc = E.op("tensor", lambda t, tt=tt: t.matmul(
                ps_bcr[:], onesf[:], stats_r[:, 0:TT], start=True, stop=True), inc=True)
            bc_done[tt] = t_bc
            E.wait("vector", "tensor", t_bc)
            tz = None
            for dc in range(NDC):
                t_sub = E.op("vector", lambda v, dc=dc: v.tensor_sub(
                    zt_f32[:, 0:TT],
                    xcache[:, xc_off + TT * dc: xc_off + TT * dc + TT], ps_bcm[:]), inc=True)
                E.wait("vector", "vector", t_sub)
                tz = E.op("vector", lambda v, dc=dc, tt=tt: v.tensor_mul(
                    z_dst[:, S * dc + TT * tt: S * dc + TT * tt + TT],
                    zt_f32[:, 0:TT], ps_bcr[:]), inc=True)
                zop[(tt, dc)] = tz
            z_done[tt] = tz
        return z_done[NTT - 1]

    def prologue_phase(x_src, z_dst, tag, nbuf=2):
        with nc.sbuf_tensor("xc" + tag, [128, nbuf * NDC * TT], BF16) as xcache, \
             nc.sbuf_tensor("xq" + tag, [128, 2 * TT], BF16) as xsqb, \
             nc.sbuf_tensor("re" + tag, [1, TT], F32) as row_e2, \
             nc.sbuf_tensor("rm" + tag, [1, TT], F32) as row_msq, \
             nc.sbuf_tensor("rv" + tag, [1, TT], F32) as row_var, \
             nc.sbuf_tensor("rl" + tag, [1, TT], F32) as row_ln, \
             nc.sbuf_tensor("zt" + tag, [128, TT], F32) as zt_f32, \
             nc.sbuf_tensor("sm" + tag, [1, TT], F32) as stats_m, \
             nc.sbuf_tensor("sr" + tag, [1, TT], F32) as stats_r, \
             nc.psum_tensor("q0" + tag, [1, TT], F32) as ps_st0, \
             nc.psum_tensor("q1" + tag, [1, TT], F32) as ps_st1, \
             nc.psum_tensor("qm" + tag, [128, TT], F32) as ps_bcm, \
             nc.psum_tensor("qr" + tag, [128, TT], F32) as ps_bcr:
            B = (xcache, xsqb, row_e2, row_msq, row_var, row_ln, zt_f32,
                 stats_m, stats_r, ps_st0, ps_st1, ps_bcm, ps_bcr)
            base[0] = run_phase(nc, sems, base[0],
                                lambda E: emit_prologue(E, x_src, z_dst, B, nbuf))

    def emit_gu(E, z, wgT, wuT, htile, sgb, pg, pu, pump=None, tread_dep=None,
                z_ticket=None):
        ev_done = {}
        mm_grp_last = {}
        h_wr = {}
        tread_of = {}
        fill_i = 0

        def emit_treads(grp):
            par = grp % 2
            t_need = tread_dep(grp) if tread_dep is not None else None
            tread_last = None
            for dc in range(NDC):
                E.wait("sync", "a", t_need)
                E.wait("sync", "tensor", mm_grp_last.get(grp - 2))
                E.tdma("s", out=wgT[:, par * 16 * TT + TT * dc: par * 16 * TT + TT * dc + TT],
                       in_=wgu_q[512 * grp:512 * grp + 512, 128 * dc:128 * dc + 128])
                tread_last = E.tdma(
                    "s", out=wuT[:, par * 16 * TT + TT * dc: par * 16 * TT + TT * dc + TT],
                    in_=wgu_q[H + 512 * grp:H + 512 * grp + 512, 128 * dc:128 * dc + 128])
            tread_of[grp] = tread_last

        for grp in range(16):
            par = grp % 2
            if grp == 0:
                emit_treads(0)
            if grp + 1 < 16:
                emit_treads(grp + 1)
            tread_last = tread_of[grp]
            for sl in range(4):
                i = 4 * grp + sl
                for ttp in range(2):
                    fp = fill_i % 2
                    tt0, tt1 = 2 * ttp, 2 * ttp + 1
                    E.wait("tensor", "s", tread_last)
                    if fill_i == 0 and z_ticket is not None:
                        E.wait("tensor", "vector", z_ticket)
                    prev = ev_done.get(fill_i - 2)
                    if prev is not None:
                        E.wait("tensor", "scalar", prev[0])
                        E.wait("tensor", "vector", prev[1])
                    t_mm = None
                    for dc in range(NDC):
                        gcol = par * 16 * TT + TT * dc + 128 * sl
                        st, sp = (dc == 0), (dc == NDC - 1)
                        E.op("tensor", lambda t, gcol=gcol, st=st, sp=sp, dc=dc, tt0=tt0, fp=fp: t.matmul(
                            pg[2 * fp][:], wgT[:, gcol:gcol + 128],
                            z[:, S * dc + TT * tt0: S * dc + TT * tt0 + TT], start=st, stop=sp))
                        E.op("tensor", lambda t, gcol=gcol, st=st, sp=sp, dc=dc, tt1=tt1, fp=fp: t.matmul(
                            pg[2 * fp + 1][:], wgT[:, gcol:gcol + 128],
                            z[:, S * dc + TT * tt1: S * dc + TT * tt1 + TT], start=st, stop=sp))
                        E.op("tensor", lambda t, gcol=gcol, st=st, sp=sp, dc=dc, tt0=tt0, fp=fp: t.matmul(
                            pu[2 * fp][:], wuT[:, gcol:gcol + 128],
                            z[:, S * dc + TT * tt0: S * dc + TT * tt0 + TT], start=st, stop=sp))
                        t_mm = E.op("tensor", lambda t, gcol=gcol, st=st, sp=sp, dc=dc, tt1=tt1, fp=fp: t.matmul(
                            pu[2 * fp + 1][:], wuT[:, gcol:gcol + 128],
                            z[:, S * dc + TT * tt1: S * dc + TT * tt1 + TT], start=st, stop=sp),
                            inc=sp)
                    if sl == 3 and ttp == 1:
                        mm_grp_last[grp] = t_mm
                    E.wait("scalar", "tensor", t_mm)
                    prev1 = ev_done.get(fill_i - 1)
                    if prev1 is not None:
                        E.wait("scalar", "vector", prev1[1])
                    E.op("scalar", lambda sc, fp=fp: sc.activation(
                        out=sgb[:, 0:TT], in_=pg[2 * fp][:], func=AF.Silu))
                    t_sg = E.op("scalar", lambda sc, fp=fp: sc.activation(
                        out=sgb[:, TT:2 * TT], in_=pg[2 * fp + 1][:], func=AF.Silu), inc=True)
                    E.wait("vector", "scalar", t_sg)
                    E.wait("vector", "tensor", t_mm)
                    E.wait("vector", "s", h_wr.get(i % 2) if i >= 2 else None)
                    hpar = i % 2
                    E.op("vector", lambda v, fp=fp, hpar=hpar, tt0=tt0: v.tensor_mul(
                        htile[:, S * hpar + TT * tt0: S * hpar + TT * tt0 + TT],
                        sgb[:, 0:TT], pu[2 * fp][:]), inc=True)
                    t_hm = E.op("vector", lambda v, fp=fp, hpar=hpar, tt1=tt1: v.tensor_mul(
                        htile[:, S * hpar + TT * tt1: S * hpar + TT * tt1 + TT],
                        sgb[:, TT:2 * TT], pu[2 * fp + 1][:]), inc=True)
                    ev_done[fill_i] = (t_sg, t_hm)
                    fill_i += 1
                    if pump is not None:
                        pump.units(1)
                E.wait("sync", "vector", ev_done[fill_i - 1][1])
                h_wr[i % 2] = E.dma("s", out=hbuf[128 * i:128 * i + 128, :],
                                    in_=htile[:, S * (i % 2):S * (i % 2) + S])
                if pump is not None and sl % 2 == 1:
                    pump.units(1)
        if pump is not None:
            pump.drain()

    def ffn_gu(x_src, z, tag):
        with ExitStack() as gu:
            esb = lambda n, sh, dt: gu.enter_context(nc.sbuf_tensor(n + tag, sh, dt))
            eps = lambda n, sh: gu.enter_context(nc.psum_tensor(n + tag, sh, F32))
            wgT = esb("wgT", [128, 2 * 16 * TT], BF16)
            wuT = esb("wuT", [128, 2 * 16 * TT], BF16)
            htile = esb("htile", [128, 2 * S], BF16)
            sgb = esb("sgb", [128, 2 * TT], F32)
            pg = [eps("pg0", [128, TT]), eps("pg1", [128, TT]),
                  eps("pg2", [128, TT]), eps("pg3", [128, TT])]
            pu = [eps("pu0", [128, TT]), eps("pu1", [128, TT]),
                  eps("pu2", [128, TT]), eps("pu3", [128, TT])]

            def gu_phase(E):
                emit_gu(E, z, wgT, wuT, htile, sgb, pg, pu)
                E.wait("gpsimd", "g", E.n["g"])
                E.wait("sync", "s", E.n["s"])

            base[0] = run_phase(nc, sems, base[0], gu_phase)

    def ffn_proj(x_src, x_dst, final, tag, x_tm=None):
        with ExitStack() as pj:
            esb = lambda n, sh, dt: pj.enter_context(nc.sbuf_tensor(n + tag, sh, dt))
            eps = lambda n, sh: pj.enter_context(nc.psum_tensor(n + tag, sh, F32))
            acc = esb("acc", [128, NDC * 1024], F32)
            hgrp = esb("hgrp", [128, 2 * 8 * 1024], BF16)
            wpT = esb("wpT", [128, 2 * 8 * D], BF16)
            hsqb = esb("hsqb", [128, 2 * 1024], BF16)
            invrow = esb("invrow", [1, 1024], F32)
            rowl = esb("rowl", [1, TT], F32)
            tmul = esb("tmul", [128, 2 * TT], F32)
            xprev = esb("xprev", [128, 2 * TT], F32)
            obuf = esb("obuf", [128, 4 * 128], F32) if final else None
            pp = [eps("pp0", [128, TT]), eps("pp1", [128, TT]),
                  eps("pp2", [128, TT]), eps("pp3", [128, TT])]
            ps_ssq = [eps("pssq0", [1, TT]), eps("pssq1", [1, TT])]
            ps_bc = eps("pbc", [128, TT])

            def proj_phase(E):
                ep_prev_v = None
                ep_inv_done = None
                ssq_prev = None
                obuf_wr = {}
                ot_ev = {}
                for th in range(2):
                    flush_t = {}
                    pe_last = {}
                    sqd = {}
                    ssq_mm = None
                    for hg in range(8):
                        gpar = hg % 2
                        wload_last = None
                        for hcl in range(8):
                            hc = 8 * hg + hcl
                            E.wait("sync", "tensor", pe_last.get((th, hg - 2)))
                            E.wait("sync", "scalar", sqd.get((th, hg - 2)))
                            E.dma("s", out=hgrp[:, (gpar * 8 + hcl) * 1024:(gpar * 8 + hcl) * 1024 + 1024],
                                  in_=hbuf[128 * hc:128 * hc + 128, 1024 * th:1024 * th + 1024])
                            wload_last = E.tdma(
                                "s", out=wpT[:, (gpar * 8 + hcl) * D:(gpar * 8 + hcl) * D + D],
                                in_=wp_q[:, 128 * hc:128 * hc + 128])
                        for hcl in range(8):
                            hld = wload_last - (7 - hcl) * 2 - 1
                            E.wait("scalar", "s", hld)
                            E.wait("scalar", "tensor", ssq_prev)
                            t_sq = E.op("scalar", lambda sc, gpar=gpar, hcl=hcl: sc.activation(
                                out=hsqb[:, (gpar) * 1024:(gpar) * 1024 + 1024],
                                in_=hgrp[:, (gpar * 8 + hcl) * 1024:(gpar * 8 + hcl) * 1024 + 1024],
                                func=AF.Square), inc=True)
                            sqd[(th, hg)] = t_sq
                            E.wait("tensor", "scalar", t_sq)
                            if th == 1 and hg == 0 and hcl == 0:
                                E.wait("tensor", "scalar", ep_inv_done)
                            for tq in range(2):
                                st = (hg == 0 and hcl == 0)
                                sp = (hg == 7 and hcl == 7)
                                t_s = E.op("tensor", lambda t, tq=tq, st=st, sp=sp, gpar=gpar, hcl=hcl: t.matmul(
                                    ps_ssq[tq][:], onesbf[:],
                                    hsqb[:, gpar * 1024 + TT * tq: gpar * 1024 + TT * tq + TT],
                                    start=st, stop=sp), inc=(tq == 1))
                                if tq == 1:
                                    ssq_prev = t_s
                                if sp:
                                    ssq_mm = t_s
                        E.wait("tensor", "s", wload_last)
                        for d16 in range(NDC):
                            prev = flush_t.get((hg, d16 - 2)) if d16 >= 2 else (
                                flush_t.get((hg - 1, NDC - 2 + d16)) if hg >= 1 else None)
                            E.wait("tensor", "vector", prev)
                            t_f = None
                            for hcl in range(8):
                                wcol = (gpar * 8 + hcl) * D + 128 * d16
                                st, sp = (hcl == 0), (hcl == 7)
                                E.op("tensor", lambda t, wcol=wcol, st=st, sp=sp, gpar=gpar, hcl=hcl, d16=d16: t.matmul(
                                    pp[2 * (d16 % 2)][:], wpT[:, wcol:wcol + 128],
                                    hgrp[:, (gpar * 8 + hcl) * 1024:(gpar * 8 + hcl) * 1024 + TT],
                                    start=st, stop=sp))
                                t_f = E.op("tensor", lambda t, wcol=wcol, st=st, sp=sp, gpar=gpar, hcl=hcl, d16=d16: t.matmul(
                                    pp[2 * (d16 % 2) + 1][:], wpT[:, wcol:wcol + 128],
                                    hgrp[:, (gpar * 8 + hcl) * 1024 + TT:(gpar * 8 + hcl) * 1024 + 2 * TT],
                                    start=st, stop=sp), inc=sp)
                            E.wait("vector", "tensor", t_f)
                            for tq in range(2):
                                col = 1024 * d16 + TT * tq
                                if hg == 0:
                                    t_fl = E.op("vector", lambda v, tq=tq, col=col, d16=d16: v.tensor_copy(
                                        out=acc[:, col:col + TT], in_=pp[2 * (d16 % 2) + tq][:]), inc=True)
                                else:
                                    t_fl = E.op("vector", lambda v, tq=tq, col=col, d16=d16: v.tensor_add(
                                        acc[:, col:col + TT], acc[:, col:col + TT],
                                        pp[2 * (d16 % 2) + tq][:]), inc=True)
                            flush_t[(hg, d16)] = t_fl
                            if d16 == NDC - 1:
                                pe_last[(th, hg)] = t_f
                    last_flush = flush_t[(7, NDC - 1)]
                    E.wait("scalar", "tensor", ssq_mm)
                    t_inv = None
                    for tq in range(2):
                        t_ln = E.op("scalar", lambda sc, tq=tq: sc.activation(
                            out=rowl[:], in_=ps_ssq[tq][:], func=AF.Ln, scale=1.0 / H, bias=epsc[0:1, 1:2]), inc=True)
                        E.wait("scalar", "scalar", t_ln)
                        t_inv = E.op("scalar", lambda sc, tq=tq: sc.activation(
                            out=invrow[:, TT * tq:TT * tq + TT], in_=rowl[:], func=AF.Exp, scale=-0.5), inc=True)
                    ep_inv_done = t_inv
                    for tq in range(2):
                        tt = 2 * th + tq
                        E.wait("tensor", "scalar", t_inv)
                        E.wait("tensor", "vector", ep_prev_v)
                        E.wait("tensor", "vector", last_flush)
                        t_bc = E.op("tensor", lambda t, tq=tq: t.matmul(
                            ps_bc[:], halvesf[:], invrow[:, TT * tq:TT * tq + TT],
                            start=True, stop=True), inc=True)
                        for d16 in range(NDC):
                            dpar = d16 % 2
                            E.wait("sync", "vector", ep_prev_v)
                            if x_tm is not None:
                                t_xl = E.tdma("s", out=xprev[:, TT * dpar:TT * dpar + TT],
                                              in_=x_tm[512 * tt:512 * tt + 512, 128 * d16:128 * d16 + 128])
                            else:
                                t_xl = E.dma("s", out=xprev[:, TT * dpar:TT * dpar + TT], in_=x_src[d16, tt])
                            E.wait("vector", "tensor", t_bc)
                            E.wait("vector", "s", t_xl)
                            E.wait("vector", "vector", last_flush)
                            E.op("vector", lambda v, d16=d16, tq=tq, dpar=dpar: v.tensor_mul(
                                tmul[:, TT * dpar:TT * dpar + TT],
                                acc[:, 1024 * d16 + TT * tq:1024 * d16 + TT * tq + TT], ps_bc[:]), inc=True)
                            t_v = E.op("vector", lambda v, dpar=dpar: v.tensor_add(
                                tmul[:, TT * dpar:TT * dpar + TT], tmul[:, TT * dpar:TT * dpar + TT],
                                xprev[:, TT * dpar:TT * dpar + TT]), inc=True)
                            ep_prev_v = t_v
                            if not final:
                                E.wait("sync", "vector", t_v)
                                E.dma("s", out=x_dst[d16, tt], in_=tmul[:, TT * dpar:TT * dpar + TT])
                            else:
                                for k in range(4):
                                    tpar = k % 2
                                    E.wait("tensor", "vector", t_v)
                                    E.wait("tensor", "scalar", ot_ev.get(tpar))
                                    t_tp = E.op("tensor", lambda t, dpar=dpar, k=k, tpar=tpar: t.transpose(
                                        pp[tpar][:, 0:128],
                                        tmul[:, TT * dpar + 128 * k:TT * dpar + 128 * k + 128],
                                        idf[:]), inc=True)
                                    E.wait("scalar", "tensor", t_tp)
                                    E.wait("scalar", "s", obuf_wr.get(k))
                                    t_ev = E.op("scalar", lambda sc, tpar=tpar, k=k: sc.copy(
                                        out=obuf[:, 128 * k:128 * k + 128], in_=pp[tpar][:, 0:128]), inc=True)
                                    ot_ev[tpar] = t_ev
                                    E.wait("sync", "scalar", t_ev)
                                    tc_glob = 4 * tt + k
                                    obuf_wr[k] = E.dma(
                                        "s", out=out[128 * tc_glob:128 * tc_glob + 128,
                                                     128 * d16:128 * d16 + 128],
                                        in_=obuf[:, 128 * k:128 * k + 128])
                E.wait("sync", "s", E.n["s"])

            base[0] = run_phase(nc, sems, base[0], proj_phase)

    if max_phase >= 1:
        with ExitStack() as ba:
            asb = lambda n, sh, dt: ba.enter_context(nc.sbuf_tensor(n, sh, dt))
            aps = lambda n: ba.enter_context(nc.psum_tensor(n, [128, TT], F32))
            xin = asb("xinA", [128, D], F32)
            xtp = asb("xtpA", [128, 4 * 128], F32)
            xtpb = asb("xtpbA", [128, 4 * 128], BF16)
            z1 = asb("z_f1", [128, NDC * S], BF16)
            xcache = asb("xcf1", [128, NDC * TT], BF16)
            xsqb = asb("xqf1", [128, 2 * TT], BF16)
            row_e2 = asb("ref1", [1, TT], F32)
            row_msq = asb("rmf1", [1, TT], F32)
            row_var = asb("rvf1", [1, TT], F32)
            row_ln = asb("rlf1", [1, TT], F32)
            zt_f32 = asb("ztf1", [128, TT], F32)
            stats_m = asb("smf1", [1, TT], F32)
            stats_r = asb("srf1", [1, TT], F32)
            Qsets = []
            for qi in range(2):
                qs = str(qi)
                Qsets.append((asb("qb" + qs, [128, 2048], BF16), asb("ta" + qs, [128, 2048], BF16),
                              asb("tb" + qs, [128, 2048], BF16), asb("qsums" + qs, [128, 32], F32),
                              asb("qsbf" + qs, [128, 32], BF16), asb("qshalf" + qs, [128, 32], BF16),
                              asb("qnshalf" + qs, [128, 32], BF16)))
            wgT = asb("wgTf1", [128, 2 * 16 * TT], BF16)
            wuT = asb("wuTf1", [128, 2 * 16 * TT], BF16)
            htile = asb("htf1", [128, 2 * S], BF16)
            sgb = asb("sgf1", [128, 2 * TT], BF16)
            pg = [aps("pgA0"), aps("pgA1"), aps("pgA2"), aps("pgA3")]
            pu = [aps("puA0"), aps("puA1"), aps("puA2"), aps("puA3")]
            p0ps = [pg[0][:, 0:128], pg[1][:, 0:128]]
            B1 = (xcache, xsqb, row_e2, row_msq, row_var, row_ln, zt_f32,
                  stats_m, stats_r, pu[0][0:1, :], pu[1][0:1, :], pg[2], pg[3])

            def blockA(E):
                pump = QuantPump(E, Qsets)
                E.dma("s", out=idf[:], in_=idf_in)
                E.dma("s", out=onesf[:], in_=onesf_in)
                E.dma("s", out=halvesf[:], in_=halvesf_in)
                E.dma("s", out=onesbf[:], in_=onesbf_in)
                E.dma("s", out=epsc[:], in_=epsc_in)
                t_c = E.dma("s", out=tri[:], in_=tri_in)
                E.wait("tensor", "s", t_c)
                ph0 = {}
                tp_i = 0
                last_ev = {}
                lastA = {}
                lastS = {}
                half_tp = {}
                t_ld = {}
                for tc in range(16):
                    E.wait("sync", "tensor", half_tp.get(tc - 1))
                    t_ld = E.dma("s", out=xin[:, 0:D],
                                 in_=x_in[128 * tc:128 * tc + 128, :])
                    for dc in range(NDC):
                        par = tp_i % 2
                        slot = tp_i % 4
                        if dc == 0:
                            E.wait("tensor", "s", t_ld)
                        E.wait("tensor", "scalar", last_ev.get(par))
                        t_mm = E.op("tensor", lambda t, dc=dc, par=par: t.transpose(
                            p0ps[par][:],
                            xin[:, 128 * dc:128 * dc + 128],
                            idf[:]), inc=True)
                        if dc % 8 == 7:
                            half_tp[tc] = t_mm
                        E.wait("scalar", "tensor", t_mm)
                        E.wait("scalar", "s", lastS.get(slot))
                        E.op("scalar", lambda sc, par=par, slot=slot: sc.copy(
                            out=xtp[:, 128 * slot:128 * slot + 128], in_=p0ps[par][:]))
                        t_ev = E.op("scalar", lambda sc, par=par, slot=slot: sc.copy(
                            out=xtpb[:, 128 * slot:128 * slot + 128], in_=p0ps[par][:]), inc=True)
                        last_ev[par] = t_ev
                        E.wait("sync", "scalar", t_ev)
                        E.dma("s", out=x0f[dc, tc // 4, :, 128 * (tc % 4):128 * (tc % 4) + 128],
                              in_=xtp[:, 128 * slot:128 * slot + 128])
                        t_wr = E.dma("s", out=x0fb[dc, tc // 4, :, 128 * (tc % 4):128 * (tc % 4) + 128],
                                     in_=xtpb[:, 128 * slot:128 * slot + 128])
                        lastS[slot] = t_wr
                        if tc % 4 == 3:
                            ph0[(dc, tc // 4)] = t_wr
                        tp_i += 1
                    pump.units(1)
                zt3 = emit_prologue(E, x0fb, z1[:], B1, nbuf=1, ph0=ph0, pump=pump,
                                    lq="s")
                emit_gu(E, z1[:], wgT, wuT, htile, sgb, pg, pu, pump=pump,
                        tread_dep=lambda g: pump.done_t[8 * g + 7], z_ticket=zt3)
                emit_staging(E)
                E.wait("sync", "s", E.n["s"])
                E.wait("scalar", "a", E.n["a"])

            base[0] = run_phase(nc, sems, base[0], blockA)
        if max_phase >= 3:
            ffn_proj(x0f, x1f, final=False, tag="f1")

    if max_phase < 4:
        gctx.close()
        return nc
    with nc.sbuf_tensor("qf", [128, 16 * S], BF16) as qf, \
         nc.sbuf_tensor("kf", [128, 4 * S], BF16) as kf, \
         nc.sbuf_tensor("vtm", [128, 16 * 520], BF16) as vtm:

        with nc.sbuf_tensor("z_at", [128, NDC * S], BF16) as z2:
            prologue_phase(x1f, z2[:], "at", nbuf=1)

            with nc.sbuf_tensor("c2s", [64, S], F32) as c2, \
                 nc.sbuf_tensor("s2s", [64, S], F32) as s2, \
                 nc.sbuf_tensor("wTq", [128, 2 * 16 * 256], BF16) as wT, \
                 nc.sbuf_tensor("ab4", [64, 4 * TT], F32) as ab4, \
                 nc.psum_tensor("pq0", [128, TT], F32) as pq0, \
                 nc.psum_tensor("pq1", [128, TT], F32) as pq1:
                pq = [pq0, pq1]

                def qkv_phase(E):
                    E.dma("s", out=c2[:], in_=c2t)
                    t_cs = E.dma("s", out=s2[:], in_=s2t)

                    last_mm_of = {}
                    carry = {"rope": None, "mm": None}

                    def proj_rope(kind, nslab, src_s, dst):
                        last_rope = {}
                        tread_of = {}
                        if carry["rope"] is not None:
                            for k2 in [(-1, 2), (-1, 3), (0, -2), (0, -1), (1, -1)]:
                                last_rope[k2] = carry["rope"]
                            last_mm_of[(kind, -2)] = carry["mm_t"]
                            last_mm_of[(kind, -1)] = carry["mm_t"]

                        def emit_tr(sg):
                            par = sg % 2
                            tl = None
                            for dc in range(NDC):
                                E.wait("sync", "g", wprep[(kind, 2 * sg + 1)])
                                E.wait("sync", "tensor", last_mm_of.get((kind, sg - 2)))
                                tl = E.tdma(
                                    "s", out=wT[:, (par * 16 + dc) * 256:(par * 16 + dc) * 256 + 256],
                                    in_=src_s[256 * sg:256 * sg + 256, 128 * dc:128 * dc + 128])
                            tread_of[sg] = tl

                        for sg in range(nslab // 2):
                            par = sg % 2
                            if sg == 0:
                                emit_tr(0)
                            if sg + 1 < nslab // 2:
                                emit_tr(sg + 1)
                            tread_last = tread_of[sg]
                            for sl in range(2):
                                s_abs = 2 * sg + sl
                                for tt in range(NTT):
                                    fp = tt % 2
                                    E.wait("tensor", "s", tread_last)
                                    prevk = (s_abs, tt - 2) if tt >= 2 else (s_abs - 1, tt + 2)
                                    E.wait("tensor", "vector", last_rope.get(prevk))
                                    if s_abs == 0 and tt < 2:
                                        E.wait("tensor", "vector", last_rope.get((-1, 2)))
                                    t_mm = None
                                    for dc in range(NDC):
                                        st, sp = (dc == 0), (dc == NDC - 1)
                                        col = (par * 16 + dc) * 256 + 128 * sl
                                        t_mm = E.op("tensor", lambda t, col=col, st=st, sp=sp, dc=dc, tt=tt, fp=fp: t.matmul(
                                            pq[fp][:], wT[:, col:col + 128],
                                            z2[:, S * dc + TT * tt: S * dc + TT * tt + TT],
                                            start=st, stop=sp), inc=sp)
                                    if tt == NTT - 1 and sl == 1:
                                        last_mm_of[(kind, sg)] = t_mm
                                    E.wait("vector", "tensor", t_mm)
                                    cs = TT * tt
                                    E.op("vector", lambda v, fp=fp, cs=cs: v.tensor_mul(
                                        ab4[:, 0:TT], pq[fp][0:64, :], c2[:, cs:cs + TT]), inc=True)
                                    E.op("vector", lambda v, fp=fp, cs=cs: v.tensor_mul(
                                        ab4[:, TT:2 * TT], pq[fp][64:128, :], s2[:, cs:cs + TT]), inc=True)
                                    E.op("vector", lambda v, fp=fp, cs=cs: v.tensor_mul(
                                        ab4[:, 2 * TT:3 * TT], pq[fp][0:64, :], s2[:, cs:cs + TT]), inc=True)
                                    t4 = E.op("vector", lambda v, fp=fp, cs=cs: v.tensor_mul(
                                        ab4[:, 3 * TT:4 * TT], pq[fp][64:128, :], c2[:, cs:cs + TT]), inc=True)
                                    E.wait("vector", "vector", t4)
                                    dcol = S * s_abs + TT * tt
                                    E.op("vector", lambda v, dcol=dcol: v.tensor_sub(
                                        dst[0:32, dcol:dcol + TT], ab4[0:32, 0:TT], ab4[0:32, TT:2 * TT]), inc=True)
                                    E.op("vector", lambda v, dcol=dcol: v.tensor_sub(
                                        dst[64:96, dcol:dcol + TT], ab4[32:64, 0:TT], ab4[32:64, TT:2 * TT]), inc=True)
                                    E.op("vector", lambda v, dcol=dcol: v.tensor_add(
                                        dst[32:64, dcol:dcol + TT], ab4[0:32, 2 * TT:3 * TT], ab4[0:32, 3 * TT:4 * TT]), inc=True)
                                    t_rp = E.op("vector", lambda v, dcol=dcol: v.tensor_add(
                                        dst[96:128, dcol:dcol + TT], ab4[32:64, 2 * TT:3 * TT], ab4[32:64, 3 * TT:4 * TT]), inc=True)
                                    last_rope[(s_abs, tt)] = t_rp
                        carry["rope"] = last_rope[(nslab - 1, NTT - 1)]
                        carry["mm_t"] = last_mm_of[(kind, nslab // 2 - 1)]

                    E.wait("vector", "s", t_cs)
                    proj_rope("q", 16, wq_s, qf[:])
                    proj_rope("k", 4, wk_s, kf[:])
                    E.wait("sync", "s", E.n["s"])
                    E.wait("gpsimd", "g", E.n["g"])

                if max_phase >= 5:
                    base[0] = run_phase(nc, sems, base[0], qkv_phase)

            with nc.sbuf_tensor("wvT", [128, 16 * TT], BF16) as wvT, \
                 nc.psum_tensor("pv0", [128, TT], F32) as pv0, \
                 nc.psum_tensor("pv1", [128, TT], F32) as pv1:
                pvs = [pv0, pv1]

                def v_phase(E):
                    t_ms = E.op("gpsimd", lambda g: g.memset(vtm[:], 1.0), inc=True)
                    t_tr = None
                    E.wait("sync", "g", wprep[("v", 0)])
                    for dc in range(NDC):
                        t_tr = E.tdma("s", out=wvT[:, TT * dc:TT * dc + TT],
                                      in_=wv_s[:, 128 * dc:128 * dc + 128])
                    ev_t = {}
                    for tc in range(16):
                        fp = tc % 2
                        E.wait("tensor", "s", t_tr)
                        E.wait("tensor", "vector", ev_t.get(tc - 2))
                        t_mm = None
                        for dc in range(NDC):
                            st, sp = (dc == 0), (dc == NDC - 1)
                            t_mm = E.op("tensor", lambda t, dc=dc, tc=tc, st=st, sp=sp, fp=fp: t.matmul(
                                pvs[fp][:], z2[:, S * dc + 128 * tc: S * dc + 128 * tc + 128],
                                wvT[:, TT * dc:TT * dc + TT], start=st, stop=sp), inc=sp)
                        E.wait("vector", "tensor", t_mm)
                        E.wait("vector", "gpsimd", t_ms)
                        ev_t[tc] = E.op("vector", lambda v, tc=tc, fp=fp: v.tensor_copy(
                            out=vtm[:, 520 * tc:520 * tc + 520].rearrange("p (g c) -> p g c", g=8)[:, :, 0:64],
                            in_=pvs[fp][:].rearrange("p (g c) -> p g c", g=8)), inc=True)

                if max_phase >= 6:
                    base[0] = run_phase(nc, sems, base[0], v_phase)

        with nc.sbuf_tensor("woT", [128, 16 * S], BF16) as woT, \
             nc.sbuf_tensor("attn", [128, 16 * TT], BF16) as attnb, \
             nc.sbuf_tensor("Pt2", [128, 2 * 1024], BF16) as Pt2, \
             nc.sbuf_tensor("sAB", [65, 4 * TT], F32) as sAB, \
             nc.sbuf_tensor("recipr", [1, 2 * TT], F32) as recipr, \
             nc.sbuf_tensor("bcs", [64, 2 * TT], F32) as bcs, \
             nc.sbuf_tensor("x1t", [128, 2 * TT], F32) as x1t, \
             nc.sbuf_tensor("x2t", [128, 2 * TT], F32) as x2t, \
             nc.psum_tensor("psS0", [128, 1024], F32) as psS0, \
             nc.psum_tensor("psS1", [128, 1024], F32) as psS1, \
             nc.psum_tensor("osA", [65, TT], F32) as osA, \
             nc.psum_tensor("osB", [65, TT], F32) as osB, \
             nc.psum_tensor("pbca", [64, TT], F32) as ps_bc:
            psS = [psS0, psS1]

            def core_phase(E):
                t_wo = None
                E.wait("sync", "g", wprep[("o", 0)])
                for dc in range(NDC):
                    t_wo = E.tdma("s", out=woT[:, S * dc:S * dc + S],
                                  in_=wo_s[:, 128 * dc:128 * dc + 128])
                exp_d = {}
                pv_d = {}
                at_wr = {}
                os_free = {}
                normq = []
                st_norm = {"prev_cpB": None, "prev_bcB": None}
                prev_oproj = None
                x2_wr = {}

                def emit_norm(task):
                    j_, p_, spar, hA_, hB_ = task
                    base = 2 * TT * spar
                    E.wait("vector", "scalar", os_free[(j_, p_)])
                    E.wait("vector", "tensor", st_norm["prev_bcB"])
                    E.op("vector", lambda v: v.reciprocal(
                        out=recipr[:, 0:TT], in_=sAB[64:65, base:base + TT]), inc=True)
                    t_rc = E.op("vector", lambda v: v.reciprocal(
                        out=recipr[:, TT:2 * TT], in_=sAB[64:65, base + TT:base + 2 * TT]), inc=True)
                    E.wait("tensor", "vector", t_rc)
                    E.wait("tensor", "scalar", st_norm["prev_cpB"])
                    t_bcA = E.op("tensor", lambda t: t.matmul(
                        ps_bc[:], onesf[0:1, 0:64], recipr[:, 0:TT], start=True, stop=True), inc=True)
                    E.wait("scalar", "tensor", t_bcA)
                    t_cpA = E.op("scalar", lambda sc: sc.copy(out=bcs[:, 0:TT], in_=ps_bc[:]), inc=True)
                    E.wait("tensor", "scalar", t_cpA)
                    t_bcB = E.op("tensor", lambda t: t.matmul(
                        ps_bc[:], onesf[0:1, 0:64], recipr[:, TT:2 * TT], start=True, stop=True), inc=True)
                    st_norm["prev_bcB"] = t_bcB
                    E.wait("scalar", "tensor", t_bcB)
                    t_cpB = E.op("scalar", lambda sc: sc.copy(out=bcs[:, TT:2 * TT], in_=ps_bc[:]), inc=True)
                    st_norm["prev_cpB"] = t_cpB
                    E.wait("vector", "scalar", t_cpB)
                    chA, baA = hA_ // 2, 64 * (hA_ % 2)
                    chB, baB = hB_ // 2, 64 * (hB_ % 2)
                    E.op("vector", lambda v, chA=chA, baA=baA, base=base: v.tensor_mul(
                        attnb[baA:baA + 64, TT * chA:TT * chA + TT],
                        sAB[0:64, base:base + TT], bcs[:, 0:TT]), inc=True)
                    t_at = E.op("vector", lambda v, chB=chB, baB=baB, base=base: v.tensor_mul(
                        attnb[baB:baB + 64, TT * chB:TT * chB + TT],
                        sAB[0:64, base + TT:base + 2 * TT], bcs[:, TT:2 * TT]), inc=True)
                    at_wr[(j_, p_)] = t_at
                for j in range(NTT):
                    for p in range(16):
                        j2, tq_ = p // 4, p % 4
                        gA, gB = 2 * j2, 2 * j2 + 1
                        hA, hB = 8 * j2 + tq_, 8 * j2 + tq_ + 4
                        nch = 4 * j + 4
                        mask_d = {}

                        def emit_pv(cc):
                            par = cc % 2
                            qlo = max(0, 128 * (cc - 4 * j))
                            ncol = TT - qlo
                            E.wait("tensor", "scalar", exp_d[(j, p, cc)])
                            E.wait("tensor", "vector", mask_d.get((j, p, cc)))
                            if cc == 0:
                                E.wait("tensor", "scalar", os_free.get((j, p - 1), os_free.get((j - 1, 15))))
                            E.op("tensor", lambda t, cc=cc, gA=gA, qlo=qlo, ncol=ncol, j=j, par=par: t.matmul(
                                osA[:, qlo:qlo + ncol], vtm[:, 520 * cc + 65 * gA:520 * cc + 65 * gA + 65],
                                Pt2[:, 1024 * par + qlo:1024 * par + qlo + ncol],
                                start=(cc == 0), stop=(cc == 4 * j + 3)))
                            t_pv = E.op("tensor", lambda t, cc=cc, gB=gB, qlo=qlo, ncol=ncol, j=j, par=par: t.matmul(
                                osB[:, qlo:qlo + ncol], vtm[:, 520 * cc + 65 * gB:520 * cc + 65 * gB + 65],
                                Pt2[:, 1024 * par + 512 + qlo:1024 * par + 512 + qlo + ncol],
                                start=(cc == 0), stop=(cc == 4 * j + 3)), inc=True)
                            pv_d[(j, p, cc)] = t_pv

                        for c in range(nch):
                            fp = c % 2
                            qlo = max(0, 128 * (c - 4 * j))
                            ncol = TT - qlo
                            E.wait("tensor", "scalar", exp_d.get((j, p, c - 2)))
                            if c < 2:
                                E.wait("tensor", "vector", prev_oproj)
                            E.op("tensor", lambda t, c=c, j2=j2, p=p, j=j, qlo=qlo, ncol=ncol, fp=fp: t.matmul(
                                psS[fp][:, qlo:qlo + ncol],
                                kf[0:64, S * j2 + 128 * c:S * j2 + 128 * c + 128],
                                qf[0:64, S * p + TT * j + qlo:S * p + TT * j + qlo + ncol],
                                start=True, stop=True, tile_position=(0, 0)), inc=True)
                            t_s2 = E.op("tensor", lambda t, c=c, j2=j2, p=p, j=j, qlo=qlo, ncol=ncol, fp=fp: t.matmul(
                                psS[fp][:, 512 + qlo:512 + qlo + ncol],
                                kf[64:128, S * j2 + 128 * c:S * j2 + 128 * c + 128],
                                qf[64:128, S * p + TT * j + qlo:S * p + TT * j + qlo + ncol],
                                start=True, stop=True, tile_position=(64, 0)), inc=True)
                            if c >= 1:
                                emit_pv(c - 1)
                            E.wait("scalar", "tensor", t_s2)
                            E.wait("scalar", "tensor", pv_d.get((j, p, c - 2)))
                            t_eB = E.op("scalar", lambda sc, qlo=qlo, fp=fp, c=c: sc.activation(
                                out=Pt2[:, 1024 * (c % 2) + qlo:1024 * (c % 2) + 1024],
                                in_=psS[fp][:, qlo:1024],
                                func=AF.Exp, scale=0.125), inc=True)
                            exp_d[(j, p, c)] = t_eB
                            if c >= 4 * j:
                                E.wait("vector", "scalar", t_eB)
                                E.op("vector", lambda v, qlo=qlo, c=c: v.tensor_mul(
                                    Pt2[:, 1024 * (c % 2) + qlo:1024 * (c % 2) + qlo + 128],
                                    Pt2[:, 1024 * (c % 2) + qlo:1024 * (c % 2) + qlo + 128], tri[:]), inc=True)
                                mask_d[(j, p, c)] = E.op("vector", lambda v, qlo=qlo, c=c: v.tensor_mul(
                                    Pt2[:, 1024 * (c % 2) + 512 + qlo:1024 * (c % 2) + 512 + qlo + 128],
                                    Pt2[:, 1024 * (c % 2) + 512 + qlo:1024 * (c % 2) + 512 + qlo + 128],
                                    tri[:]), inc=True)
                        emit_pv(nch - 1)
                        t_pvl = pv_d[(j, p, nch - 1)]
                        spar = p % 2
                        E.wait("scalar", "tensor", t_pvl)
                        E.wait("scalar", "vector", at_wr.get((j, p - 2), at_wr.get((j - 1, 14 + p))))
                        E.op("scalar", lambda sc, spar=spar: sc.copy(
                            out=sAB[:, 2 * TT * spar:2 * TT * spar + TT], in_=osA[:]))
                        t_ev = E.op("scalar", lambda sc, spar=spar: sc.copy(
                            out=sAB[:, 2 * TT * spar + TT:2 * TT * spar + 2 * TT], in_=osB[:]), inc=True)
                        os_free[(j, p)] = t_ev
                        normq.append((j, p, spar, hA, hB))
                        if len(normq) > 1:
                            emit_norm(normq.pop(0))
                    while normq:
                        emit_norm(normq.pop(0))
                    E.wait("tensor", "s", t_wo)
                    oproj_ev = {}
                    for d16 in range(NDC):
                        fp2 = d16 % 2
                        E.wait("tensor", "vector", at_wr[(j, 15)])
                        E.wait("tensor", "vector", oproj_ev.get(d16 - 2))
                        t_mo = None
                        for dc in range(NDC):
                            st, sp = (dc == 0), (dc == NDC - 1)
                            t_mo = E.op("tensor", lambda t, dc=dc, d16=d16, st=st, sp=sp, fp2=fp2: t.matmul(
                                psS[fp2][:, 0:TT], woT[:, S * dc + 128 * d16:S * dc + 128 * d16 + 128],
                                attnb[:, TT * dc:TT * dc + TT], start=st, stop=sp), inc=sp)
                        E.wait("sync", "vector", oproj_ev.get(d16 - 2))
                        t_xl = E.dma("s", out=x1t[:, TT * fp2:TT * fp2 + TT], in_=x1f[d16, j])
                        E.wait("vector", "tensor", t_mo)
                        E.wait("vector", "s", t_xl)
                        E.wait("vector", "s", x2_wr.get(fp2))
                        t_x2 = E.op("vector", lambda v, fp2=fp2: v.tensor_add(
                            x2t[:, TT * fp2:TT * fp2 + TT], psS[fp2][:, 0:TT],
                            x1t[:, TT * fp2:TT * fp2 + TT]), inc=True)
                        oproj_ev[d16] = t_x2
                        E.wait("sync", "vector", t_x2)
                        x2_wr[fp2] = E.dma("s", out=x2f[d16, j], in_=x2t[:, TT * fp2:TT * fp2 + TT])
                    prev_oproj = oproj_ev[NDC - 1]
                E.wait("sync", "s", E.n["s"])

            if max_phase >= 7:
                base[0] = run_phase(nc, sems, base[0], core_phase)

    if max_phase >= 8:
        with nc.sbuf_tensor("z_f2", [128, NDC * S], BF16) as z3:
            prologue_phase(x2f, z3[:], "f2")
            if max_phase >= 9:
                ffn_gu(x2f, z3[:], tag="f2")
        if max_phase >= 10:
            ffn_proj(x2f, None, final=True, tag="f2")

    gctx.close()
    return nc


_NC_CACHE = None


def _host_constants():
    t = np.arange(S, dtype=np.float32)
    inv_freq = (1.0 / np.power(np.float32(10000.0),
                               np.arange(0, 64, 2, dtype=np.float32) / np.float32(64.0))).astype(np.float32)
    freqs = np.outer(t, inv_freq).astype(np.float32)
    cos = np.cos(freqs).astype(np.float32).T
    sin = np.sin(freqs).astype(np.float32).T
    c2 = np.ascontiguousarray(np.concatenate([cos, cos], axis=0))
    s2 = np.ascontiguousarray(np.concatenate([sin, sin], axis=0))
    kidx = np.arange(128)[:, None]
    qidx = np.arange(128)[None, :]
    tri = (qidx >= kidx).astype(bfd)
    onesf = np.ones((1, 128), np.float32)
    halvesf = np.full((1, 128), 0.5, np.float32)
    onesbf = np.ones((128, 1), bfd)
    idf = np.eye(128, dtype=np.float32)
    return c2, s2, tri, onesf, halvesf, onesbf, idf


def kernel(L0_attn_x, L0_attn_wq, L0_attn_wk, L0_attn_wv, L0_attn_wo,
           L0_ffn_w_gate_up, L0_ffn_w_proj):
    global _NC_CACHE
    if _NC_CACHE is None:
        _NC_CACHE = build_program()
    nc = _NC_CACHE
    c2, s2, tri, onesf, halvesf, onesbf, idf = _host_constants()
    basem = {
        "wq": np.ascontiguousarray(np.asarray(L0_attn_wq), np.float32),
        "wk": np.ascontiguousarray(np.asarray(L0_attn_wk), np.float32),
        "wv": np.ascontiguousarray(np.asarray(L0_attn_wv), np.float32),
        "wo": np.ascontiguousarray(np.asarray(L0_attn_wo), np.float32),
        "wgu": np.ascontiguousarray(np.asarray(L0_ffn_w_gate_up), np.float32),
        "wp": np.ascontiguousarray(np.asarray(L0_ffn_w_proj), np.float32),
        "c2t": c2, "s2t": s2, "tri": tri, "onesf": onesf,
        "halvesf": halvesf, "onesbf": onesbf, "idf": idf,
        "epsc": np.array([[EPS_LN, EPS_RMS]], np.float32),
    }
    xfull = np.asarray(L0_attn_x)
    in_maps = []
    for b in range(8):
        m = dict(basem)
        m["x"] = np.ascontiguousarray(xfull[b], np.float32)
        in_maps.append(m)
    res = run_bass_kernel_spmd(nc, in_maps, core_ids=list(range(8)))
    return np.stack([np.asarray(r["out"], np.float32) for r in res.results], axis=0)


# version 43
# speedup vs baseline: 1.0087x; 1.0087x over previous
import numpy as np
import ml_dtypes
import concourse.bass as bass
import concourse.mybir as mybir
from concourse.bass_utils import run_bass_kernel_spmd
from contextlib import ExitStack

F32 = mybir.dt.float32
BF16 = mybir.dt.bfloat16
AF = mybir.ActivationFunctionType
OP = mybir.AluOpType
AX = mybir.AxisListType
bfd = ml_dtypes.bfloat16

D = 2048
S = 2048
NDC = 16
NTT = 4
TT = 512
H = 8192
EPS_LN = 1e-5
EPS_RMS = float(np.finfo(np.float32).eps)


class Emitter:

    def __init__(self, which, handle, sems, base):
        self.which = which
        self.h = handle
        self.sems = sems
        self.n = dict(base)

    def dma(self, q, out=None, in_=None, **kw):
        eng = {"g": "gpsimd", "s": "sync", "a": "scalar", "v": "gpsimd",
               "w": "gpsimd"}[q]
        self.n[q] += 1
        if self.which == eng:
            self.h.dma_start(out=out, in_=in_, **kw).then_inc(self.sems[q], 16)
        return self.n[q]

    def tdma(self, q, out, in_):
        eng = {"g": "gpsimd", "s": "sync", "a": "scalar", "v": "gpsimd",
               "w": "gpsimd"}[q]
        self.n[q] += 1
        if self.which == eng:
            self.h.dma_start_transpose(out=out, in_=in_).then_inc(self.sems[q], 16)
        return self.n[q]

    def op(self, eng, fn, inc=False):
        if self.which == eng:
            inst = fn(self.h)
            if inc:
                inst.then_inc(self.sems[eng], 1)
        if inc:
            self.n[eng] += 1
            return self.n[eng]
        return None

    def wait(self, eng, sem_name, ticket):
        if ticket is None or (isinstance(ticket, int) and ticket <= 0):
            return
        if self.which == eng:
            mult = 16 if sem_name in ("g", "s", "a", "v", "w") else 1
            self.h.wait_ge(self.sems[sem_name], mult * ticket)


def run_phase(nc, sems, base, phase_fn):
    finals = {}
    with nc.Block() as block:
        def mk(which):
            def body(handle):
                E = Emitter(which, handle, sems, base)
                phase_fn(E)
                finals[which] = dict(E.n)
            return body
        block.tensor(mk("tensor"))
        block.vector(mk("vector"))
        block.scalar(mk("scalar"))
        block.gpsimd(mk("gpsimd"))
        block.sync(mk("sync"))
    f0 = finals["tensor"]
    for k, v in finals.items():
        assert v == f0, f"ticket divergence {k}: {v} vs {f0}"
    return f0


def build_program(max_phase=99):
    nc = bass.Bass(trn_type="TRN2", target_bir_lowering=False)

    x_in = nc.dram_tensor("x", [S, D], F32, kind="ExternalInput").ap()
    wq = nc.dram_tensor("wq", [D, D], F32, kind="ExternalInput").ap()
    wk = nc.dram_tensor("wk", [512, D], F32, kind="ExternalInput").ap()
    wv = nc.dram_tensor("wv", [512, D], F32, kind="ExternalInput").ap()
    wo = nc.dram_tensor("wo", [D, D], F32, kind="ExternalInput").ap()
    wgu = nc.dram_tensor("wgu", [2 * H, D], F32, kind="ExternalInput").ap()
    wp = nc.dram_tensor("wp", [D, H], F32, kind="ExternalInput").ap()
    c2t = nc.dram_tensor("c2t", [64, S], F32, kind="ExternalInput").ap()
    s2t = nc.dram_tensor("s2t", [64, S], F32, kind="ExternalInput").ap()
    tri_in = nc.dram_tensor("tri", [128, 128], BF16, kind="ExternalInput").ap()
    onesf_in = nc.dram_tensor("onesf", [1, 128], F32, kind="ExternalInput").ap()
    halvesf_in = nc.dram_tensor("halvesf", [1, 128], F32, kind="ExternalInput").ap()
    onesbf_in = nc.dram_tensor("onesbf", [128, 1], BF16, kind="ExternalInput").ap()
    idf_in = nc.dram_tensor("idf", [128, 128], F32, kind="ExternalInput").ap()
    epsc_in = nc.dram_tensor("epsc", [1, 2], F32, kind="ExternalInput").ap()
    out = nc.dram_tensor("out", [S, D], F32, kind="ExternalOutput").ap()

    x0f = nc.dram_tensor("x0f", [NDC, NTT, 128, TT], F32, kind="Internal").ap()
    x0fb = nc.dram_tensor("x0fb", [NDC, NTT, 128, TT], BF16, kind="Internal").ap()
    x1f = nc.dram_tensor("x1f", [NDC, NTT, 128, TT], F32, kind="Internal").ap()
    x2f = nc.dram_tensor("x2f", [NDC, NTT, 128, TT], F32, kind="Internal").ap()
    hbuf = nc.dram_tensor("hbuf", [H, S], BF16, kind="Internal").ap()
    wgu_q = nc.dram_tensor("wgu_q", [2 * H, D], BF16, kind="Internal").ap()
    wp_q = nc.dram_tensor("wp_q", [D, H], BF16, kind="Internal").ap()
    wq_s = nc.dram_tensor("wq_s", [D, D], BF16, kind="Internal").ap()
    wk_s = nc.dram_tensor("wk_s", [512, D], BF16, kind="Internal").ap()
    wv_s = nc.dram_tensor("wv_s", [512, D], BF16, kind="Internal").ap()
    wo_s = nc.dram_tensor("wo_s", [D, D], BF16, kind="Internal").ap()

    gctx = ExitStack()
    gsb = lambda name, shape, dt: gctx.enter_context(nc.sbuf_tensor(name, shape, dt))
    idf = gsb("idf_s", [128, 128], F32)
    onesf = gsb("onesf_s", [1, 128], F32)
    halvesf = gsb("halvesf_s", [1, 128], F32)
    onesbf = gsb("onesbf_s", [128, 1], BF16)
    tri = gsb("tri_s", [128, 128], BF16)
    epsc = gsb("epsc_s", [1, 2], F32)

    sems = {}
    for name in ("g", "s", "a", "v", "w", "tensor", "vector", "scalar", "gpsimd"):
        sems[name] = gctx.enter_context(nc.semaphore(f"sem_{name}"))
    base = [{k: 0 for k in ("g", "s", "a", "v", "w",
                            "tensor", "vector", "scalar", "gpsimd")}]

    class QuantPump:
        def __init__(self, E, Qsets):
            self.E = E
            self.Q = Qsets
            self.done_t = {}
            self.wr = [None, None]
            self.loaded = {}
            self.u = 0
            self.NU = 192

        def _params(self, u):
            if u < 128:
                half, idx = u % 2, u // 2
                r0 = (H if half else 0) + 128 * idx
                return (wgu[r0:r0 + 128, :], wgu_q[r0:r0 + 128, :])
            v = u - 128
            q, slab = v // 16, v % 16
            return (wp[128 * slab:128 * slab + 128, 2048 * q:2048 * q + 2048],
                    wp_q[128 * slab:128 * slab + 128, 2048 * q:2048 * q + 2048])

        def _emit_load(self, u):
            if u >= self.NU or u in self.loaded:
                return
            E = self.E
            s = u % 2
            qb = self.Q[s][0]
            src, _ = self._params(u)
            E.wait("gpsimd", "a", self.wr[s])
            self.loaded[u] = (("v", "w")[s], E.dma(("v", "w")[s], out=qb[:], in_=src))

        def _emit_unit(self, u):
            E = self.E
            s = u % 2
            qb, ta, tb, qsums, qsbf, qshalf, qnshalf = self.Q[s]
            src, dst = self._params(u)
            self._emit_load(u)
            lq_in, t_in = self.loaded[u]
            self._emit_load(u + 1)
            E.wait("vector", lq_in, t_in)
            t_red = E.op("vector", lambda v: v.tensor_reduce(
                out=qsums[:, 0:32], in_=qb[:].rearrange("p (g k) -> p g k", k=64),
                axis=AX.X, op=OP.add, apply_absolute_value=True), inc=True)
            E.wait("scalar", "vector", t_red)
            t_mul = E.op("scalar", lambda sc: sc.mul(qsbf[:, 0:32], qsums[:, 0:32], 1.0 / 64.0), inc=True)
            E.wait("vector", "scalar", t_mul)
            t = E.op("vector", lambda v: v.tensor_scalar_max(qsbf[:, 0:32], qsbf[:, 0:32], 1e-8), inc=True)
            E.wait("vector", "vector", t)
            E.op("vector", lambda v: v.tensor_scalar_mul(qshalf[:, 0:32], qsbf[:, 0:32], 0.5), inc=True)
            t = E.op("vector", lambda v: v.tensor_scalar_mul(qnshalf[:, 0:32], qsbf[:, 0:32], -0.5), inc=True)
            E.wait("vector", "vector", t)
            E.op("vector", lambda v: v.tensor_tensor(
                out=ta[:].rearrange("p (g k) -> p g k", k=64),
                in0=qb[:].rearrange("p (g k) -> p g k", k=64),
                in1=qshalf[:, 0:32].unsqueeze(2).broadcast_to([128, 32, 64]), op=OP.is_gt), inc=True)
            t = E.op("vector", lambda v: v.tensor_tensor(
                out=tb[:].rearrange("p (g k) -> p g k", k=64),
                in0=qb[:].rearrange("p (g k) -> p g k", k=64),
                in1=qnshalf[:, 0:32].unsqueeze(2).broadcast_to([128, 32, 64]), op=OP.is_lt), inc=True)
            E.wait("vector", "vector", t)
            t = E.op("vector", lambda v: v.tensor_sub(qb[:], ta[:], tb[:]), inc=True)
            E.wait("vector", "vector", t)
            t_q = E.op("vector", lambda v: v.tensor_tensor(
                out=ta[:].rearrange("p (g k) -> p g k", k=64),
                in0=qb[:].rearrange("p (g k) -> p g k", k=64),
                in1=qsbf[:, 0:32].unsqueeze(2).broadcast_to([128, 32, 64]), op=OP.mult), inc=True)
            E.wait("scalar", "vector", t_q)
            t_wr = E.dma("a", out=dst, in_=ta[:])
            self.wr[s] = t_wr
            self.done_t[u] = t_wr

        def units(self, n):
            for _ in range(n):
                if self.u >= self.NU:
                    return
                self._emit_unit(self.u)
                self.u += 1

        def drain(self):
            self.units(self.NU - self.u)

    wprep = {}

    def emit_staging(E):
        wqv = wq.rearrange("(j2 head t pair par) d -> j2 t par head pair d",
                           j2=4, head=2, t=4, pair=32, par=2)
        for s_ in range(16):
            tt_ = None
            for blkk in range(4):
                par_, head_ = blkk // 2, blkk % 2
                tt_ = E.dma("g", out=wq_s[128 * s_ + 32 * blkk:128 * s_ + 32 * blkk + 32, :],
                            in_=wqv[s_ // 4, s_ % 4, par_, head_])
            wprep[("q", s_)] = tt_
        wkv = wk.rearrange("(j2 head pair par) d -> j2 par head pair d",
                           j2=4, head=2, pair=32, par=2)
        for s_ in range(4):
            tt_ = None
            for blkk in range(4):
                par_, head_ = blkk // 2, blkk % 2
                tt_ = E.dma("g", out=wk_s[128 * s_ + 32 * blkk:128 * s_ + 32 * blkk + 32, :],
                            in_=wkv[s_, par_, head_])
            wprep[("k", s_)] = tt_
        wprep[("v", 0)] = E.dma("g", out=wv_s, in_=wv)
        wprep[("o", 0)] = E.dma("g", out=wo_s, in_=wo)

    def emit_prologue(E, x_src, z_dst, B, nbuf=2, ph0=None, pump=None, lq="g"):
        (xcache, xsqb, row_e2, row_msq, row_var, row_ln, zt_f32,
         stats_m, stats_r, ps_st0, ps_st1, ps_bcm, ps_bcr) = B
        le = {"g": "gpsimd", "s": "sync"}[lq]
        z_done = {}
        zop = {}
        sq_of = {}
        srow_done = {}
        bc_done = {}
        for tt in range(NTT):
            xc_off = (tt % nbuf) * (NDC * TT)
            st1_of_dc = {}
            for dc in range(NDC):
                if pump is not None and dc % 2 == 1:
                    pump.units(1)
                E.wait(le, "vector", zop.get((tt - nbuf, dc)))
                E.wait(le, "scalar", sq_of.get((tt - nbuf, dc)))
                if ph0 is not None:
                    E.wait(le, "s", ph0[(dc, tt)])
                t_ld = E.dma(lq, out=xcache[:, xc_off + TT * dc: xc_off + TT * dc + TT],
                             in_=x_src[dc, tt])
                E.wait("scalar", lq, t_ld)
                E.wait("scalar", "tensor", st1_of_dc.get(dc - 2))
                t_sq = E.op("scalar", lambda sc, dc=dc: sc.activation(
                    out=xsqb[:, TT * (dc % 2):TT * (dc % 2) + TT],
                    in_=xcache[:, xc_off + TT * dc: xc_off + TT * dc + TT],
                    func=AF.Square), inc=True)
                sq_of[(tt, dc)] = t_sq
                E.wait("tensor", lq, t_ld)
                E.wait("tensor", "scalar", t_sq)
                if dc == 0:
                    E.wait("tensor", "scalar", srow_done.get(tt - 1))
                E.op("tensor", lambda t, dc=dc: t.matmul(
                    ps_st0[:], onesbf[:],
                    xcache[:, xc_off + TT * dc: xc_off + TT * dc + TT],
                    start=(dc == 0), stop=(dc == NDC - 1)))
                t_mm = E.op("tensor", lambda t, dc=dc: t.matmul(
                    ps_st1[:], onesbf[:],
                    xsqb[:, TT * (dc % 2):TT * (dc % 2) + TT],
                    start=(dc == 0), stop=(dc == NDC - 1)), inc=True)
                st1_of_dc[dc] = t_mm
            E.wait("scalar", "tensor", t_mm)
            E.wait("scalar", "tensor", bc_done.get(tt - 1))
            E.op("scalar", lambda sc, tt=tt: sc.mul(stats_m[:, 0:TT], ps_st0[:], 1.0 / D))
            t_e2 = E.op("scalar", lambda sc: sc.mul(row_e2[:], ps_st1[:], 1.0 / D), inc=True)
            t_msq = E.op("scalar", lambda sc, tt=tt: sc.activation(
                out=row_msq[:], in_=stats_m[:, 0:TT], func=AF.Square), inc=True)
            srow_done[tt] = t_msq
            E.wait("vector", "scalar", t_msq)
            t_var = E.op("vector", lambda v: v.tensor_sub(row_var[:], row_e2[:], row_msq[:]), inc=True)
            E.wait("scalar", "vector", t_var)
            t_ln = E.op("scalar", lambda sc: sc.activation(
                out=row_ln[:], in_=row_var[:], func=AF.Ln, bias=epsc[0:1, 0:1]), inc=True)
            E.wait("scalar", "scalar", t_ln)
            t_rs = E.op("scalar", lambda sc, tt=tt: sc.activation(
                out=stats_r[:, 0:TT], in_=row_ln[:], func=AF.Exp, scale=-0.5), inc=True)
            E.wait("tensor", "scalar", t_rs)
            E.wait("tensor", "vector", z_done.get(tt - 1))
            E.op("tensor", lambda t, tt=tt: t.matmul(
                ps_bcm[:], onesf[:], stats_m[:, 0:TT], start=True, stop=True))
            t_bc = E.op("tensor", lambda t, tt=tt: t.matmul(
                ps_bcr[:], onesf[:], stats_r[:, 0:TT], start=True, stop=True), inc=True)
            bc_done[tt] = t_bc
            E.wait("vector", "tensor", t_bc)
            tz = None
            for dc in range(NDC):
                t_sub = E.op("vector", lambda v, dc=dc: v.tensor_sub(
                    zt_f32[:, 0:TT],
                    xcache[:, xc_off + TT * dc: xc_off + TT * dc + TT], ps_bcm[:]), inc=True)
                E.wait("vector", "vector", t_sub)
                tz = E.op("vector", lambda v, dc=dc, tt=tt: v.tensor_mul(
                    z_dst[:, S * dc + TT * tt: S * dc + TT * tt + TT],
                    zt_f32[:, 0:TT], ps_bcr[:]), inc=True)
                zop[(tt, dc)] = tz
            z_done[tt] = tz
        return z_done[NTT - 1]

    def prologue_phase(x_src, z_dst, tag, nbuf=2):
        with nc.sbuf_tensor("xc" + tag, [128, nbuf * NDC * TT], BF16) as xcache, \
             nc.sbuf_tensor("xq" + tag, [128, 2 * TT], BF16) as xsqb, \
             nc.sbuf_tensor("re" + tag, [1, TT], F32) as row_e2, \
             nc.sbuf_tensor("rm" + tag, [1, TT], F32) as row_msq, \
             nc.sbuf_tensor("rv" + tag, [1, TT], F32) as row_var, \
             nc.sbuf_tensor("rl" + tag, [1, TT], F32) as row_ln, \
             nc.sbuf_tensor("zt" + tag, [128, TT], F32) as zt_f32, \
             nc.sbuf_tensor("sm" + tag, [1, TT], F32) as stats_m, \
             nc.sbuf_tensor("sr" + tag, [1, TT], F32) as stats_r, \
             nc.psum_tensor("q0" + tag, [1, TT], F32) as ps_st0, \
             nc.psum_tensor("q1" + tag, [1, TT], F32) as ps_st1, \
             nc.psum_tensor("qm" + tag, [128, TT], F32) as ps_bcm, \
             nc.psum_tensor("qr" + tag, [128, TT], F32) as ps_bcr:
            B = (xcache, xsqb, row_e2, row_msq, row_var, row_ln, zt_f32,
                 stats_m, stats_r, ps_st0, ps_st1, ps_bcm, ps_bcr)
            base[0] = run_phase(nc, sems, base[0],
                                lambda E: emit_prologue(E, x_src, z_dst, B, nbuf))

    def emit_gu(E, z, wgT, wuT, htile, sgb, pg, pu, pump=None, tread_dep=None,
                z_ticket=None):
        ev_done = {}
        mm_grp_last = {}
        h_wr = {}
        tread_of = {}
        fill_i = 0

        def emit_treads(grp):
            par = grp % 2
            t_need = tread_dep(grp) if tread_dep is not None else None
            tread_last = None
            for dc in range(NDC):
                E.wait("sync", "a", t_need)
                E.wait("sync", "tensor", mm_grp_last.get(grp - 2))
                E.tdma("s", out=wgT[:, par * 16 * TT + TT * dc: par * 16 * TT + TT * dc + TT],
                       in_=wgu_q[512 * grp:512 * grp + 512, 128 * dc:128 * dc + 128])
                tread_last = E.tdma(
                    "s", out=wuT[:, par * 16 * TT + TT * dc: par * 16 * TT + TT * dc + TT],
                    in_=wgu_q[H + 512 * grp:H + 512 * grp + 512, 128 * dc:128 * dc + 128])
            tread_of[grp] = tread_last

        for grp in range(16):
            par = grp % 2
            if grp == 0:
                emit_treads(0)
            if grp + 1 < 16:
                emit_treads(grp + 1)
            tread_last = tread_of[grp]
            for sl in range(4):
                i = 4 * grp + sl
                for ttp in range(2):
                    fp = fill_i % 2
                    tt0, tt1 = 2 * ttp, 2 * ttp + 1
                    E.wait("tensor", "s", tread_last)
                    if fill_i == 0 and z_ticket is not None:
                        E.wait("tensor", "vector", z_ticket)
                    prev = ev_done.get(fill_i - 2)
                    if prev is not None:
                        E.wait("tensor", "scalar", prev[0])
                        E.wait("tensor", "vector", prev[1])
                    t_mm = None
                    for dc in range(NDC):
                        gcol = par * 16 * TT + TT * dc + 128 * sl
                        st, sp = (dc == 0), (dc == NDC - 1)
                        E.op("tensor", lambda t, gcol=gcol, st=st, sp=sp, dc=dc, tt0=tt0, fp=fp: t.matmul(
                            pg[2 * fp][:], wgT[:, gcol:gcol + 128],
                            z[:, S * dc + TT * tt0: S * dc + TT * tt0 + TT], start=st, stop=sp))
                        E.op("tensor", lambda t, gcol=gcol, st=st, sp=sp, dc=dc, tt1=tt1, fp=fp: t.matmul(
                            pg[2 * fp + 1][:], wgT[:, gcol:gcol + 128],
                            z[:, S * dc + TT * tt1: S * dc + TT * tt1 + TT], start=st, stop=sp))
                        E.op("tensor", lambda t, gcol=gcol, st=st, sp=sp, dc=dc, tt0=tt0, fp=fp: t.matmul(
                            pu[2 * fp][:], wuT[:, gcol:gcol + 128],
                            z[:, S * dc + TT * tt0: S * dc + TT * tt0 + TT], start=st, stop=sp))
                        t_mm = E.op("tensor", lambda t, gcol=gcol, st=st, sp=sp, dc=dc, tt1=tt1, fp=fp: t.matmul(
                            pu[2 * fp + 1][:], wuT[:, gcol:gcol + 128],
                            z[:, S * dc + TT * tt1: S * dc + TT * tt1 + TT], start=st, stop=sp),
                            inc=sp)
                    if sl == 3 and ttp == 1:
                        mm_grp_last[grp] = t_mm
                    E.wait("scalar", "tensor", t_mm)
                    prev1 = ev_done.get(fill_i - 1)
                    if prev1 is not None:
                        E.wait("scalar", "vector", prev1[1])
                    E.op("scalar", lambda sc, fp=fp: sc.activation(
                        out=sgb[:, 0:TT], in_=pg[2 * fp][:], func=AF.Silu))
                    t_sg = E.op("scalar", lambda sc, fp=fp: sc.activation(
                        out=sgb[:, TT:2 * TT], in_=pg[2 * fp + 1][:], func=AF.Silu), inc=True)
                    E.wait("vector", "scalar", t_sg)
                    E.wait("vector", "tensor", t_mm)
                    E.wait("vector", "s", h_wr.get(i % 2) if i >= 2 else None)
                    hpar = i % 2
                    E.op("vector", lambda v, fp=fp, hpar=hpar, tt0=tt0: v.tensor_mul(
                        htile[:, S * hpar + TT * tt0: S * hpar + TT * tt0 + TT],
                        sgb[:, 0:TT], pu[2 * fp][:]), inc=True)
                    t_hm = E.op("vector", lambda v, fp=fp, hpar=hpar, tt1=tt1: v.tensor_mul(
                        htile[:, S * hpar + TT * tt1: S * hpar + TT * tt1 + TT],
                        sgb[:, TT:2 * TT], pu[2 * fp + 1][:]), inc=True)
                    ev_done[fill_i] = (t_sg, t_hm)
                    fill_i += 1
                    if pump is not None:
                        pump.units(1)
                E.wait("sync", "vector", ev_done[fill_i - 1][1])
                h_wr[i % 2] = E.dma("s", out=hbuf[128 * i:128 * i + 128, :],
                                    in_=htile[:, S * (i % 2):S * (i % 2) + S])
                if pump is not None and sl % 2 == 1:
                    pump.units(1)
        if pump is not None:
            pump.drain()

    def ffn_gu(x_src, z, tag):
        with ExitStack() as gu:
            esb = lambda n, sh, dt: gu.enter_context(nc.sbuf_tensor(n + tag, sh, dt))
            eps = lambda n, sh: gu.enter_context(nc.psum_tensor(n + tag, sh, F32))
            wgT = esb("wgT", [128, 2 * 16 * TT], BF16)
            wuT = esb("wuT", [128, 2 * 16 * TT], BF16)
            htile = esb("htile", [128, 2 * S], BF16)
            sgb = esb("sgb", [128, 2 * TT], F32)
            pg = [eps("pg0", [128, TT]), eps("pg1", [128, TT]),
                  eps("pg2", [128, TT]), eps("pg3", [128, TT])]
            pu = [eps("pu0", [128, TT]), eps("pu1", [128, TT]),
                  eps("pu2", [128, TT]), eps("pu3", [128, TT])]

            def gu_phase(E):
                emit_gu(E, z, wgT, wuT, htile, sgb, pg, pu)
                E.wait("gpsimd", "g", E.n["g"])
                E.wait("sync", "s", E.n["s"])

            base[0] = run_phase(nc, sems, base[0], gu_phase)

    def ffn_proj(x_src, x_dst, final, tag, x_tm=None):
        with ExitStack() as pj:
            esb = lambda n, sh, dt: pj.enter_context(nc.sbuf_tensor(n + tag, sh, dt))
            eps = lambda n, sh: pj.enter_context(nc.psum_tensor(n + tag, sh, F32))
            acc = esb("acc", [128, NDC * 1024], F32)
            hgrp = esb("hgrp", [128, 2 * 8 * 1024], BF16)
            wpT = esb("wpT", [128, 2 * 8 * D], BF16)
            hsqb = esb("hsqb", [128, 2 * 1024], BF16)
            invrow = esb("invrow", [1, 1024], F32)
            rowl = esb("rowl", [1, TT], F32)
            tmul = esb("tmul", [128, 2 * TT], F32)
            xprev = esb("xprev", [128, 2 * TT], F32)
            obuf = esb("obuf", [128, 4 * 128], F32) if final else None
            pp = [eps("pp0", [128, TT]), eps("pp1", [128, TT]),
                  eps("pp2", [128, TT]), eps("pp3", [128, TT])]
            ps_ssq = [eps("pssq0", [1, TT]), eps("pssq1", [1, TT])]
            ps_bc = eps("pbc", [128, TT])

            def proj_phase(E):
                ep_prev_v = None
                ep_inv_done = None
                ssq_prev = None
                obuf_wr = {}
                ot_ev = {}
                for th in range(2):
                    flush_t = {}
                    pe_last = {}
                    sqd = {}
                    ssq_mm = None
                    for hg in range(8):
                        gpar = hg % 2
                        wload_last = None
                        for hcl in range(8):
                            hc = 8 * hg + hcl
                            E.wait("sync", "tensor", pe_last.get((th, hg - 2)))
                            E.wait("sync", "scalar", sqd.get((th, hg - 2)))
                            E.dma("s", out=hgrp[:, (gpar * 8 + hcl) * 1024:(gpar * 8 + hcl) * 1024 + 1024],
                                  in_=hbuf[128 * hc:128 * hc + 128, 1024 * th:1024 * th + 1024])
                            wload_last = E.tdma(
                                "s", out=wpT[:, (gpar * 8 + hcl) * D:(gpar * 8 + hcl) * D + D],
                                in_=wp_q[:, 128 * hc:128 * hc + 128])
                        for hcl in range(8):
                            hld = wload_last - (7 - hcl) * 2 - 1
                            E.wait("scalar", "s", hld)
                            E.wait("scalar", "tensor", ssq_prev)
                            t_sq = E.op("scalar", lambda sc, gpar=gpar, hcl=hcl: sc.activation(
                                out=hsqb[:, (gpar) * 1024:(gpar) * 1024 + 1024],
                                in_=hgrp[:, (gpar * 8 + hcl) * 1024:(gpar * 8 + hcl) * 1024 + 1024],
                                func=AF.Square), inc=True)
                            sqd[(th, hg)] = t_sq
                            E.wait("tensor", "scalar", t_sq)
                            if th == 1 and hg == 0 and hcl == 0:
                                E.wait("tensor", "scalar", ep_inv_done)
                            for tq in range(2):
                                st = (hg == 0 and hcl == 0)
                                sp = (hg == 7 and hcl == 7)
                                t_s = E.op("tensor", lambda t, tq=tq, st=st, sp=sp, gpar=gpar, hcl=hcl: t.matmul(
                                    ps_ssq[tq][:], onesbf[:],
                                    hsqb[:, gpar * 1024 + TT * tq: gpar * 1024 + TT * tq + TT],
                                    start=st, stop=sp), inc=(tq == 1))
                                if tq == 1:
                                    ssq_prev = t_s
                                if sp:
                                    ssq_mm = t_s
                        E.wait("tensor", "s", wload_last)
                        for d16 in range(NDC):
                            prev = flush_t.get((hg, d16 - 2)) if d16 >= 2 else (
                                flush_t.get((hg - 1, NDC - 2 + d16)) if hg >= 1 else None)
                            E.wait("tensor", "vector", prev)
                            t_f = None
                            for hcl in range(8):
                                wcol = (gpar * 8 + hcl) * D + 128 * d16
                                st, sp = (hcl == 0), (hcl == 7)
                                E.op("tensor", lambda t, wcol=wcol, st=st, sp=sp, gpar=gpar, hcl=hcl, d16=d16: t.matmul(
                                    pp[2 * (d16 % 2)][:], wpT[:, wcol:wcol + 128],
                                    hgrp[:, (gpar * 8 + hcl) * 1024:(gpar * 8 + hcl) * 1024 + TT],
                                    start=st, stop=sp))
                                t_f = E.op("tensor", lambda t, wcol=wcol, st=st, sp=sp, gpar=gpar, hcl=hcl, d16=d16: t.matmul(
                                    pp[2 * (d16 % 2) + 1][:], wpT[:, wcol:wcol + 128],
                                    hgrp[:, (gpar * 8 + hcl) * 1024 + TT:(gpar * 8 + hcl) * 1024 + 2 * TT],
                                    start=st, stop=sp), inc=sp)
                            E.wait("vector", "tensor", t_f)
                            for tq in range(2):
                                col = 1024 * d16 + TT * tq
                                if hg == 0:
                                    t_fl = E.op("vector", lambda v, tq=tq, col=col, d16=d16: v.tensor_copy(
                                        out=acc[:, col:col + TT], in_=pp[2 * (d16 % 2) + tq][:]), inc=True)
                                else:
                                    t_fl = E.op("vector", lambda v, tq=tq, col=col, d16=d16: v.tensor_add(
                                        acc[:, col:col + TT], acc[:, col:col + TT],
                                        pp[2 * (d16 % 2) + tq][:]), inc=True)
                            flush_t[(hg, d16)] = t_fl
                            if d16 == NDC - 1:
                                pe_last[(th, hg)] = t_f
                    last_flush = flush_t[(7, NDC - 1)]
                    E.wait("scalar", "tensor", ssq_mm)
                    t_inv = None
                    for tq in range(2):
                        t_ln = E.op("scalar", lambda sc, tq=tq: sc.activation(
                            out=rowl[:], in_=ps_ssq[tq][:], func=AF.Ln, scale=1.0 / H, bias=epsc[0:1, 1:2]), inc=True)
                        E.wait("scalar", "scalar", t_ln)
                        t_inv = E.op("scalar", lambda sc, tq=tq: sc.activation(
                            out=invrow[:, TT * tq:TT * tq + TT], in_=rowl[:], func=AF.Exp, scale=-0.5), inc=True)
                    ep_inv_done = t_inv
                    for tq in range(2):
                        tt = 2 * th + tq
                        E.wait("tensor", "scalar", t_inv)
                        E.wait("tensor", "vector", ep_prev_v)
                        E.wait("tensor", "vector", last_flush)
                        t_bc = E.op("tensor", lambda t, tq=tq: t.matmul(
                            ps_bc[:], halvesf[:], invrow[:, TT * tq:TT * tq + TT],
                            start=True, stop=True), inc=True)
                        for d16 in range(NDC):
                            dpar = d16 % 2
                            E.wait("sync", "vector", ep_prev_v)
                            if x_tm is not None:
                                t_xl = E.tdma("s", out=xprev[:, TT * dpar:TT * dpar + TT],
                                              in_=x_tm[512 * tt:512 * tt + 512, 128 * d16:128 * d16 + 128])
                            else:
                                t_xl = E.dma("s", out=xprev[:, TT * dpar:TT * dpar + TT], in_=x_src[d16, tt])
                            E.wait("vector", "tensor", t_bc)
                            E.wait("vector", "s", t_xl)
                            E.wait("vector", "vector", last_flush)
                            E.op("vector", lambda v, d16=d16, tq=tq, dpar=dpar: v.tensor_mul(
                                tmul[:, TT * dpar:TT * dpar + TT],
                                acc[:, 1024 * d16 + TT * tq:1024 * d16 + TT * tq + TT], ps_bc[:]), inc=True)
                            t_v = E.op("vector", lambda v, dpar=dpar: v.tensor_add(
                                tmul[:, TT * dpar:TT * dpar + TT], tmul[:, TT * dpar:TT * dpar + TT],
                                xprev[:, TT * dpar:TT * dpar + TT]), inc=True)
                            ep_prev_v = t_v
                            if not final:
                                E.wait("sync", "vector", t_v)
                                E.dma("s", out=x_dst[d16, tt], in_=tmul[:, TT * dpar:TT * dpar + TT])
                            else:
                                for k in range(4):
                                    tpar = k % 2
                                    E.wait("tensor", "vector", t_v)
                                    E.wait("tensor", "scalar", ot_ev.get(tpar))
                                    t_tp = E.op("tensor", lambda t, dpar=dpar, k=k, tpar=tpar: t.transpose(
                                        pp[tpar][:, 0:128],
                                        tmul[:, TT * dpar + 128 * k:TT * dpar + 128 * k + 128],
                                        idf[:]), inc=True)
                                    E.wait("scalar", "tensor", t_tp)
                                    E.wait("scalar", "s", obuf_wr.get(k))
                                    t_ev = E.op("scalar", lambda sc, tpar=tpar, k=k: sc.copy(
                                        out=obuf[:, 128 * k:128 * k + 128], in_=pp[tpar][:, 0:128]), inc=True)
                                    ot_ev[tpar] = t_ev
                                    E.wait("sync", "scalar", t_ev)
                                    tc_glob = 4 * tt + k
                                    obuf_wr[k] = E.dma(
                                        "s", out=out[128 * tc_glob:128 * tc_glob + 128,
                                                     128 * d16:128 * d16 + 128],
                                        in_=obuf[:, 128 * k:128 * k + 128])
                E.wait("sync", "s", E.n["s"])

            base[0] = run_phase(nc, sems, base[0], proj_phase)

    if max_phase >= 1:
        with ExitStack() as ba:
            asb = lambda n, sh, dt: ba.enter_context(nc.sbuf_tensor(n, sh, dt))
            aps = lambda n: ba.enter_context(nc.psum_tensor(n, [128, TT], F32))
            xin = asb("xinA", [128, D], F32)
            xtp = asb("xtpA", [128, 4 * 128], F32)
            xtpb = asb("xtpbA", [128, 4 * 128], BF16)
            z1 = asb("z_f1", [128, NDC * S], BF16)
            xcache = asb("xcf1", [128, NDC * TT], BF16)
            xsqb = asb("xqf1", [128, 2 * TT], BF16)
            row_e2 = asb("ref1", [1, TT], F32)
            row_msq = asb("rmf1", [1, TT], F32)
            row_var = asb("rvf1", [1, TT], F32)
            row_ln = asb("rlf1", [1, TT], F32)
            zt_f32 = asb("ztf1", [128, TT], F32)
            stats_m = asb("smf1", [1, TT], F32)
            stats_r = asb("srf1", [1, TT], F32)
            Qsets = []
            for qi in range(2):
                qs = str(qi)
                Qsets.append((asb("qb" + qs, [128, 2048], BF16), asb("ta" + qs, [128, 2048], BF16),
                              asb("tb" + qs, [128, 2048], BF16), asb("qsums" + qs, [128, 32], F32),
                              asb("qsbf" + qs, [128, 32], BF16), asb("qshalf" + qs, [128, 32], BF16),
                              asb("qnshalf" + qs, [128, 32], BF16)))
            wgT = asb("wgTf1", [128, 2 * 16 * TT], BF16)
            wuT = asb("wuTf1", [128, 2 * 16 * TT], BF16)
            htile = asb("htf1", [128, 2 * S], BF16)
            sgb = asb("sgf1", [128, 2 * TT], BF16)
            pg = [aps("pgA0"), aps("pgA1"), aps("pgA2"), aps("pgA3")]
            pu = [aps("puA0"), aps("puA1"), aps("puA2"), aps("puA3")]
            p0ps = [pg[0][:, 0:128], pg[1][:, 0:128]]
            B1 = (xcache, xsqb, row_e2, row_msq, row_var, row_ln, zt_f32,
                  stats_m, stats_r, pu[0][0:1, :], pu[1][0:1, :], pg[2], pg[3])

            def blockA(E):
                pump = QuantPump(E, Qsets)
                E.dma("s", out=idf[:], in_=idf_in)
                E.dma("s", out=onesf[:], in_=onesf_in)
                E.dma("s", out=halvesf[:], in_=halvesf_in)
                E.dma("s", out=onesbf[:], in_=onesbf_in)
                E.dma("s", out=epsc[:], in_=epsc_in)
                t_c = E.dma("s", out=tri[:], in_=tri_in)
                E.wait("tensor", "s", t_c)
                ph0 = {}
                tp_i = 0
                last_ev = {}
                lastA = {}
                lastS = {}
                half_tp = {}
                t_ld = {}
                for tc in range(16):
                    E.wait("sync", "tensor", half_tp.get(tc - 1))
                    t_ld = E.dma("s", out=xin[:, 0:D],
                                 in_=x_in[128 * tc:128 * tc + 128, :])
                    for dc in range(NDC):
                        par = tp_i % 2
                        slot = tp_i % 4
                        if dc == 0:
                            E.wait("tensor", "s", t_ld)
                        E.wait("tensor", "scalar", last_ev.get(par))
                        t_mm = E.op("tensor", lambda t, dc=dc, par=par: t.transpose(
                            p0ps[par][:],
                            xin[:, 128 * dc:128 * dc + 128],
                            idf[:]), inc=True)
                        if dc % 8 == 7:
                            half_tp[tc] = t_mm
                        E.wait("scalar", "tensor", t_mm)
                        E.wait("scalar", "s", lastS.get(slot))
                        E.op("scalar", lambda sc, par=par, slot=slot: sc.copy(
                            out=xtp[:, 128 * slot:128 * slot + 128], in_=p0ps[par][:]))
                        t_ev = E.op("scalar", lambda sc, par=par, slot=slot: sc.copy(
                            out=xtpb[:, 128 * slot:128 * slot + 128], in_=p0ps[par][:]), inc=True)
                        last_ev[par] = t_ev
                        E.wait("sync", "scalar", t_ev)
                        E.dma("s", out=x0f[dc, tc // 4, :, 128 * (tc % 4):128 * (tc % 4) + 128],
                              in_=xtp[:, 128 * slot:128 * slot + 128])
                        t_wr = E.dma("s", out=x0fb[dc, tc // 4, :, 128 * (tc % 4):128 * (tc % 4) + 128],
                                     in_=xtpb[:, 128 * slot:128 * slot + 128])
                        lastS[slot] = t_wr
                        if tc % 4 == 3:
                            ph0[(dc, tc // 4)] = t_wr
                        tp_i += 1
                    pump.units(1)
                zt3 = emit_prologue(E, x0fb, z1[:], B1, nbuf=1, ph0=ph0, pump=pump,
                                    lq="s")
                emit_gu(E, z1[:], wgT, wuT, htile, sgb, pg, pu, pump=pump,
                        tread_dep=lambda g: pump.done_t[8 * g + 7], z_ticket=zt3)
                emit_staging(E)
                E.wait("sync", "s", E.n["s"])
                E.wait("scalar", "a", E.n["a"])

            base[0] = run_phase(nc, sems, base[0], blockA)
        if max_phase >= 3:
            ffn_proj(x0f, x1f, final=False, tag="f1")

    if max_phase < 4:
        gctx.close()
        return nc
    with nc.sbuf_tensor("qf", [128, 16 * S], BF16) as qf, \
         nc.sbuf_tensor("kf", [128, 4 * S], BF16) as kf, \
         nc.sbuf_tensor("vtm", [128, 16 * 520], BF16) as vtm:

        with nc.sbuf_tensor("z_at", [128, NDC * S], BF16) as z2:
            prologue_phase(x1f, z2[:], "at", nbuf=1)

            with nc.sbuf_tensor("c2s", [64, S], F32) as c2, \
                 nc.sbuf_tensor("s2s", [64, S], F32) as s2, \
                 nc.sbuf_tensor("wTq", [128, 2 * 16 * 256], BF16) as wT, \
                 nc.sbuf_tensor("ab4", [64, 4 * TT], F32) as ab4, \
                 nc.psum_tensor("pq0", [128, TT], F32) as pq0, \
                 nc.psum_tensor("pq1", [128, TT], F32) as pq1:
                pq = [pq0, pq1]

                def qkv_phase(E):
                    E.dma("s", out=c2[:], in_=c2t)
                    t_cs = E.dma("s", out=s2[:], in_=s2t)

                    last_mm_of = {}
                    carry = {"rope": None, "mm": None}

                    def proj_rope(kind, nslab, src_s, dst):
                        last_rope = {}
                        tread_of = {}
                        if carry["rope"] is not None:
                            for k2 in [(-1, 2), (-1, 3), (0, -2), (0, -1), (1, -1)]:
                                last_rope[k2] = carry["rope"]
                            last_mm_of[(kind, -2)] = carry["mm_t"]
                            last_mm_of[(kind, -1)] = carry["mm_t"]

                        def emit_tr(sg):
                            par = sg % 2
                            tl = None
                            for dc in range(NDC):
                                E.wait("sync", "g", wprep[(kind, 2 * sg + 1)])
                                E.wait("sync", "tensor", last_mm_of.get((kind, sg - 2)))
                                tl = E.tdma(
                                    "s", out=wT[:, (par * 16 + dc) * 256:(par * 16 + dc) * 256 + 256],
                                    in_=src_s[256 * sg:256 * sg + 256, 128 * dc:128 * dc + 128])
                            tread_of[sg] = tl

                        for sg in range(nslab // 2):
                            par = sg % 2
                            if sg == 0:
                                emit_tr(0)
                            if sg + 1 < nslab // 2:
                                emit_tr(sg + 1)
                            tread_last = tread_of[sg]
                            for sl in range(2):
                                s_abs = 2 * sg + sl
                                for tt in range(NTT):
                                    fp = tt % 2
                                    E.wait("tensor", "s", tread_last)
                                    prevk = (s_abs, tt - 2) if tt >= 2 else (s_abs - 1, tt + 2)
                                    E.wait("tensor", "vector", last_rope.get(prevk))
                                    if s_abs == 0 and tt < 2:
                                        E.wait("tensor", "vector", last_rope.get((-1, 2)))
                                    t_mm = None
                                    for dc in range(NDC):
                                        st, sp = (dc == 0), (dc == NDC - 1)
                                        col = (par * 16 + dc) * 256 + 128 * sl
                                        t_mm = E.op("tensor", lambda t, col=col, st=st, sp=sp, dc=dc, tt=tt, fp=fp: t.matmul(
                                            pq[fp][:], wT[:, col:col + 128],
                                            z2[:, S * dc + TT * tt: S * dc + TT * tt + TT],
                                            start=st, stop=sp), inc=sp)
                                    if tt == NTT - 1 and sl == 1:
                                        last_mm_of[(kind, sg)] = t_mm
                                    E.wait("vector", "tensor", t_mm)
                                    cs = TT * tt
                                    E.op("vector", lambda v, fp=fp, cs=cs: v.tensor_mul(
                                        ab4[:, 0:TT], pq[fp][0:64, :], c2[:, cs:cs + TT]), inc=True)
                                    E.op("vector", lambda v, fp=fp, cs=cs: v.tensor_mul(
                                        ab4[:, TT:2 * TT], pq[fp][64:128, :], s2[:, cs:cs + TT]), inc=True)
                                    E.op("vector", lambda v, fp=fp, cs=cs: v.tensor_mul(
                                        ab4[:, 2 * TT:3 * TT], pq[fp][0:64, :], s2[:, cs:cs + TT]), inc=True)
                                    t4 = E.op("vector", lambda v, fp=fp, cs=cs: v.tensor_mul(
                                        ab4[:, 3 * TT:4 * TT], pq[fp][64:128, :], c2[:, cs:cs + TT]), inc=True)
                                    E.wait("vector", "vector", t4)
                                    dcol = S * s_abs + TT * tt
                                    E.op("vector", lambda v, dcol=dcol: v.tensor_sub(
                                        dst[0:32, dcol:dcol + TT], ab4[0:32, 0:TT], ab4[0:32, TT:2 * TT]), inc=True)
                                    E.op("vector", lambda v, dcol=dcol: v.tensor_sub(
                                        dst[64:96, dcol:dcol + TT], ab4[32:64, 0:TT], ab4[32:64, TT:2 * TT]), inc=True)
                                    E.op("vector", lambda v, dcol=dcol: v.tensor_add(
                                        dst[32:64, dcol:dcol + TT], ab4[0:32, 2 * TT:3 * TT], ab4[0:32, 3 * TT:4 * TT]), inc=True)
                                    t_rp = E.op("vector", lambda v, dcol=dcol: v.tensor_add(
                                        dst[96:128, dcol:dcol + TT], ab4[32:64, 2 * TT:3 * TT], ab4[32:64, 3 * TT:4 * TT]), inc=True)
                                    last_rope[(s_abs, tt)] = t_rp
                        carry["rope"] = last_rope[(nslab - 1, NTT - 1)]
                        carry["mm_t"] = last_mm_of[(kind, nslab // 2 - 1)]

                    E.wait("vector", "s", t_cs)
                    proj_rope("q", 16, wq_s, qf[:])
                    proj_rope("k", 4, wk_s, kf[:])
                    E.wait("sync", "s", E.n["s"])
                    E.wait("gpsimd", "g", E.n["g"])

                if max_phase >= 5:
                    base[0] = run_phase(nc, sems, base[0], qkv_phase)

            with nc.sbuf_tensor("wvT", [128, 16 * TT], BF16) as wvT, \
                 nc.psum_tensor("pv0", [128, TT], F32) as pv0, \
                 nc.psum_tensor("pv1", [128, TT], F32) as pv1:
                pvs = [pv0, pv1]

                def v_phase(E):
                    t_ms = E.op("gpsimd", lambda g: g.memset(vtm[:], 1.0), inc=True)
                    t_tr = None
                    E.wait("sync", "g", wprep[("v", 0)])
                    for dc in range(NDC):
                        t_tr = E.tdma("s", out=wvT[:, TT * dc:TT * dc + TT],
                                      in_=wv_s[:, 128 * dc:128 * dc + 128])
                    ev_t = {}
                    for tc in range(16):
                        fp = tc % 2
                        E.wait("tensor", "s", t_tr)
                        E.wait("tensor", "vector", ev_t.get(tc - 2))
                        t_mm = None
                        for dc in range(NDC):
                            st, sp = (dc == 0), (dc == NDC - 1)
                            t_mm = E.op("tensor", lambda t, dc=dc, tc=tc, st=st, sp=sp, fp=fp: t.matmul(
                                pvs[fp][:], z2[:, S * dc + 128 * tc: S * dc + 128 * tc + 128],
                                wvT[:, TT * dc:TT * dc + TT], start=st, stop=sp), inc=sp)
                        E.wait("vector", "tensor", t_mm)
                        E.wait("vector", "gpsimd", t_ms)
                        ev_t[tc] = E.op("vector", lambda v, tc=tc, fp=fp: v.tensor_copy(
                            out=vtm[:, 520 * tc:520 * tc + 520].rearrange("p (g c) -> p g c", g=8)[:, :, 0:64],
                            in_=pvs[fp][:].rearrange("p (g c) -> p g c", g=8)), inc=True)

                if max_phase >= 6:
                    base[0] = run_phase(nc, sems, base[0], v_phase)

        with nc.sbuf_tensor("woT", [128, 16 * S], BF16) as woT, \
             nc.sbuf_tensor("attn", [128, 16 * TT], BF16) as attnb, \
             nc.sbuf_tensor("Pt2", [128, 2 * 1024], BF16) as Pt2, \
             nc.sbuf_tensor("sAB", [65, 4 * TT], F32) as sAB, \
             nc.sbuf_tensor("recipr", [1, 2 * TT], F32) as recipr, \
             nc.sbuf_tensor("bcs", [64, 2 * TT], F32) as bcs, \
             nc.sbuf_tensor("x1t", [128, 2 * TT], F32) as x1t, \
             nc.sbuf_tensor("x2t", [128, 2 * TT], F32) as x2t, \
             nc.psum_tensor("psS0", [128, 1024], F32) as psS0, \
             nc.psum_tensor("psS1", [128, 1024], F32) as psS1, \
             nc.psum_tensor("osA", [65, TT], F32) as osA, \
             nc.psum_tensor("osB", [65, TT], F32) as osB, \
             nc.psum_tensor("pbca", [64, TT], F32) as ps_bc:
            psS = [psS0, psS1]

            def core_phase(E):
                t_wo = None
                E.wait("sync", "g", wprep[("o", 0)])
                for dc in range(NDC):
                    t_wo = E.tdma("s", out=woT[:, S * dc:S * dc + S],
                                  in_=wo_s[:, 128 * dc:128 * dc + 128])
                exp_d = {}
                pv_d = {}
                at_wr = {}
                os_free = {}
                normq = []
                st_norm = {"prev_cpB": None, "prev_bcB": None}
                prev_oproj = None
                x2_wr = {}

                def emit_norm(task):
                    j_, p_, spar, hA_, hB_ = task
                    base = 2 * TT * spar
                    E.wait("vector", "scalar", os_free[(j_, p_)])
                    E.wait("vector", "tensor", st_norm["prev_bcB"])
                    E.op("vector", lambda v: v.reciprocal(
                        out=recipr[:, 0:TT], in_=sAB[64:65, base:base + TT]), inc=True)
                    t_rc = E.op("vector", lambda v: v.reciprocal(
                        out=recipr[:, TT:2 * TT], in_=sAB[64:65, base + TT:base + 2 * TT]), inc=True)
                    E.wait("tensor", "vector", t_rc)
                    E.wait("tensor", "scalar", st_norm["prev_cpB"])
                    t_bcA = E.op("tensor", lambda t: t.matmul(
                        ps_bc[:], onesf[0:1, 0:64], recipr[:, 0:TT], start=True, stop=True), inc=True)
                    E.wait("scalar", "tensor", t_bcA)
                    t_cpA = E.op("scalar", lambda sc: sc.copy(out=bcs[:, 0:TT], in_=ps_bc[:]), inc=True)
                    E.wait("tensor", "scalar", t_cpA)
                    t_bcB = E.op("tensor", lambda t: t.matmul(
                        ps_bc[:], onesf[0:1, 0:64], recipr[:, TT:2 * TT], start=True, stop=True), inc=True)
                    st_norm["prev_bcB"] = t_bcB
                    E.wait("scalar", "tensor", t_bcB)
                    t_cpB = E.op("scalar", lambda sc: sc.copy(out=bcs[:, TT:2 * TT], in_=ps_bc[:]), inc=True)
                    st_norm["prev_cpB"] = t_cpB
                    E.wait("vector", "scalar", t_cpB)
                    chA, baA = hA_ // 2, 64 * (hA_ % 2)
                    chB, baB = hB_ // 2, 64 * (hB_ % 2)
                    E.op("vector", lambda v, chA=chA, baA=baA, base=base: v.tensor_mul(
                        attnb[baA:baA + 64, TT * chA:TT * chA + TT],
                        sAB[0:64, base:base + TT], bcs[:, 0:TT]), inc=True)
                    t_at = E.op("vector", lambda v, chB=chB, baB=baB, base=base: v.tensor_mul(
                        attnb[baB:baB + 64, TT * chB:TT * chB + TT],
                        sAB[0:64, base + TT:base + 2 * TT], bcs[:, TT:2 * TT]), inc=True)
                    at_wr[(j_, p_)] = t_at
                for j in range(NTT):
                    for p in range(16):
                        j2, tq_ = p // 4, p % 4
                        gA, gB = 2 * j2, 2 * j2 + 1
                        hA, hB = 8 * j2 + tq_, 8 * j2 + tq_ + 4
                        nch = 4 * j + 4
                        mask_d = {}

                        def emit_pv(cc):
                            par = cc % 2
                            qlo = max(0, 128 * (cc - 4 * j))
                            ncol = TT - qlo
                            E.wait("tensor", "scalar", exp_d[(j, p, cc)])
                            E.wait("tensor", "vector", mask_d.get((j, p, cc)))
                            if cc == 0:
                                E.wait("tensor", "scalar", os_free.get((j, p - 1), os_free.get((j - 1, 15))))
                            E.op("tensor", lambda t, cc=cc, gA=gA, qlo=qlo, ncol=ncol, j=j, par=par: t.matmul(
                                osA[:, qlo:qlo + ncol], vtm[:, 520 * cc + 65 * gA:520 * cc + 65 * gA + 65],
                                Pt2[:, 1024 * par + qlo:1024 * par + qlo + ncol],
                                start=(cc == 0), stop=(cc == 4 * j + 3)))
                            t_pv = E.op("tensor", lambda t, cc=cc, gB=gB, qlo=qlo, ncol=ncol, j=j, par=par: t.matmul(
                                osB[:, qlo:qlo + ncol], vtm[:, 520 * cc + 65 * gB:520 * cc + 65 * gB + 65],
                                Pt2[:, 1024 * par + 512 + qlo:1024 * par + 512 + qlo + ncol],
                                start=(cc == 0), stop=(cc == 4 * j + 3)), inc=True)
                            pv_d[(j, p, cc)] = t_pv

                        for c in range(nch):
                            fp = c % 2
                            qlo = max(0, 128 * (c - 4 * j))
                            ncol = TT - qlo
                            E.wait("tensor", "scalar", exp_d.get((j, p, c - 2)))
                            if c < 2:
                                E.wait("tensor", "vector", prev_oproj)
                            E.op("tensor", lambda t, c=c, j2=j2, p=p, j=j, qlo=qlo, ncol=ncol, fp=fp: t.matmul(
                                psS[fp][:, qlo:qlo + ncol],
                                kf[0:64, S * j2 + 128 * c:S * j2 + 128 * c + 128],
                                qf[0:64, S * p + TT * j + qlo:S * p + TT * j + qlo + ncol],
                                start=True, stop=True, tile_position=(0, 0)), inc=True)
                            t_s2 = E.op("tensor", lambda t, c=c, j2=j2, p=p, j=j, qlo=qlo, ncol=ncol, fp=fp: t.matmul(
                                psS[fp][:, 512 + qlo:512 + qlo + ncol],
                                kf[64:128, S * j2 + 128 * c:S * j2 + 128 * c + 128],
                                qf[64:128, S * p + TT * j + qlo:S * p + TT * j + qlo + ncol],
                                start=True, stop=True, tile_position=(64, 0)), inc=True)
                            if c >= 1:
                                emit_pv(c - 1)
                            E.wait("scalar", "tensor", t_s2)
                            E.wait("scalar", "tensor", pv_d.get((j, p, c - 2)))
                            t_eB = E.op("scalar", lambda sc, qlo=qlo, fp=fp, c=c: sc.activation(
                                out=Pt2[:, 1024 * (c % 2) + qlo:1024 * (c % 2) + 1024],
                                in_=psS[fp][:, qlo:1024],
                                func=AF.Exp, scale=0.125), inc=True)
                            exp_d[(j, p, c)] = t_eB
                            if c >= 4 * j:
                                E.wait("vector", "scalar", t_eB)
                                E.op("vector", lambda v, qlo=qlo, c=c: v.tensor_mul(
                                    Pt2[:, 1024 * (c % 2) + qlo:1024 * (c % 2) + qlo + 128],
                                    Pt2[:, 1024 * (c % 2) + qlo:1024 * (c % 2) + qlo + 128], tri[:]), inc=True)
                                mask_d[(j, p, c)] = E.op("vector", lambda v, qlo=qlo, c=c: v.tensor_mul(
                                    Pt2[:, 1024 * (c % 2) + 512 + qlo:1024 * (c % 2) + 512 + qlo + 128],
                                    Pt2[:, 1024 * (c % 2) + 512 + qlo:1024 * (c % 2) + 512 + qlo + 128],
                                    tri[:]), inc=True)
                        emit_pv(nch - 1)
                        t_pvl = pv_d[(j, p, nch - 1)]
                        spar = p % 2
                        E.wait("scalar", "tensor", t_pvl)
                        E.wait("scalar", "vector", at_wr.get((j, p - 2), at_wr.get((j - 1, 14 + p))))
                        E.op("scalar", lambda sc, spar=spar: sc.copy(
                            out=sAB[:, 2 * TT * spar:2 * TT * spar + TT], in_=osA[:]))
                        t_ev = E.op("scalar", lambda sc, spar=spar: sc.copy(
                            out=sAB[:, 2 * TT * spar + TT:2 * TT * spar + 2 * TT], in_=osB[:]), inc=True)
                        os_free[(j, p)] = t_ev
                        normq.append((j, p, spar, hA, hB))
                        if len(normq) > 1:
                            emit_norm(normq.pop(0))
                    while normq:
                        emit_norm(normq.pop(0))
                    E.wait("tensor", "s", t_wo)
                    oproj_ev = {}
                    for d16 in range(NDC):
                        fp2 = d16 % 2
                        E.wait("tensor", "vector", at_wr[(j, 15)])
                        E.wait("tensor", "vector", oproj_ev.get(d16 - 2))
                        t_mo = None
                        for dc in range(NDC):
                            st, sp = (dc == 0), (dc == NDC - 1)
                            t_mo = E.op("tensor", lambda t, dc=dc, d16=d16, st=st, sp=sp, fp2=fp2: t.matmul(
                                psS[fp2][:, 0:TT], woT[:, S * dc + 128 * d16:S * dc + 128 * d16 + 128],
                                attnb[:, TT * dc:TT * dc + TT], start=st, stop=sp), inc=sp)
                        E.wait("sync", "vector", oproj_ev.get(d16 - 2))
                        t_xl = E.dma("s", out=x1t[:, TT * fp2:TT * fp2 + TT], in_=x1f[d16, j])
                        E.wait("vector", "tensor", t_mo)
                        E.wait("vector", "s", t_xl)
                        E.wait("vector", "s", x2_wr.get(fp2))
                        t_x2 = E.op("vector", lambda v, fp2=fp2: v.tensor_add(
                            x2t[:, TT * fp2:TT * fp2 + TT], psS[fp2][:, 0:TT],
                            x1t[:, TT * fp2:TT * fp2 + TT]), inc=True)
                        oproj_ev[d16] = t_x2
                        E.wait("sync", "vector", t_x2)
                        x2_wr[fp2] = E.dma("s", out=x2f[d16, j], in_=x2t[:, TT * fp2:TT * fp2 + TT])
                    prev_oproj = oproj_ev[NDC - 1]
                E.wait("sync", "s", E.n["s"])

            if max_phase >= 7:
                base[0] = run_phase(nc, sems, base[0], core_phase)

    if max_phase >= 8:
        with nc.sbuf_tensor("z_f2", [128, NDC * S], BF16) as z3:
            prologue_phase(x2f, z3[:], "f2")
            if max_phase >= 9:
                ffn_gu(x2f, z3[:], tag="f2")
        if max_phase >= 10:
            ffn_proj(x2f, None, final=True, tag="f2")

    gctx.close()
    return nc


_NC_CACHE = None


def _host_constants():
    t = np.arange(S, dtype=np.float32)
    inv_freq = (1.0 / np.power(np.float32(10000.0),
                               np.arange(0, 64, 2, dtype=np.float32) / np.float32(64.0))).astype(np.float32)
    freqs = np.outer(t, inv_freq).astype(np.float32)
    cos = np.cos(freqs).astype(np.float32).T
    sin = np.sin(freqs).astype(np.float32).T
    c2 = np.ascontiguousarray(np.concatenate([cos, cos], axis=0))
    s2 = np.ascontiguousarray(np.concatenate([sin, sin], axis=0))
    kidx = np.arange(128)[:, None]
    qidx = np.arange(128)[None, :]
    tri = (qidx >= kidx).astype(bfd)
    onesf = np.ones((1, 128), np.float32)
    halvesf = np.full((1, 128), 0.5, np.float32)
    onesbf = np.ones((128, 1), bfd)
    idf = np.eye(128, dtype=np.float32)
    return c2, s2, tri, onesf, halvesf, onesbf, idf


def kernel(L0_attn_x, L0_attn_wq, L0_attn_wk, L0_attn_wv, L0_attn_wo,
           L0_ffn_w_gate_up, L0_ffn_w_proj):
    global _NC_CACHE
    if _NC_CACHE is None:
        _NC_CACHE = build_program()
    nc = _NC_CACHE
    c2, s2, tri, onesf, halvesf, onesbf, idf = _host_constants()
    basem = {
        "wq": np.ascontiguousarray(np.asarray(L0_attn_wq), np.float32),
        "wk": np.ascontiguousarray(np.asarray(L0_attn_wk), np.float32),
        "wv": np.ascontiguousarray(np.asarray(L0_attn_wv), np.float32),
        "wo": np.ascontiguousarray(np.asarray(L0_attn_wo), np.float32),
        "wgu": np.ascontiguousarray(np.asarray(L0_ffn_w_gate_up), np.float32),
        "wp": np.ascontiguousarray(np.asarray(L0_ffn_w_proj), np.float32),
        "c2t": c2, "s2t": s2, "tri": tri, "onesf": onesf,
        "halvesf": halvesf, "onesbf": onesbf, "idf": idf,
        "epsc": np.array([[EPS_LN, EPS_RMS]], np.float32),
    }
    xfull = np.asarray(L0_attn_x)
    in_maps = []
    for b in range(8):
        m = dict(basem)
        m["x"] = np.ascontiguousarray(xfull[b], np.float32)
        in_maps.append(m)
    res = run_bass_kernel_spmd(nc, in_maps, core_ids=list(range(8)))
    return np.stack([np.asarray(r["out"], np.float32) for r in res.results], axis=0)


# version 44 (final)
# speedup vs baseline: 1.0338x; 1.0249x over previous
import numpy as np
import ml_dtypes
import concourse.bass as bass
import concourse.mybir as mybir
from concourse.bass_utils import run_bass_kernel_spmd
from contextlib import ExitStack

F32 = mybir.dt.float32
BF16 = mybir.dt.bfloat16
AF = mybir.ActivationFunctionType
OP = mybir.AluOpType
AX = mybir.AxisListType
bfd = ml_dtypes.bfloat16

D = 2048
S = 2048
NDC = 16
NTT = 4
TT = 512
H = 8192
EPS_LN = 1e-5
EPS_RMS = float(np.finfo(np.float32).eps)


class Emitter:

    def __init__(self, which, handle, sems, base):
        self.which = which
        self.h = handle
        self.sems = sems
        self.n = dict(base)

    def dma(self, q, out=None, in_=None, **kw):
        eng = {"g": "gpsimd", "s": "sync", "a": "scalar", "v": "gpsimd"}[q]
        self.n[q] += 1
        if self.which == eng:
            self.h.dma_start(out=out, in_=in_, **kw).then_inc(self.sems[q], 16)
        return self.n[q]

    def tdma(self, q, out, in_):
        eng = {"g": "gpsimd", "s": "sync", "a": "scalar", "v": "gpsimd"}[q]
        self.n[q] += 1
        if self.which == eng:
            self.h.dma_start_transpose(out=out, in_=in_).then_inc(self.sems[q], 16)
        return self.n[q]

    def op(self, eng, fn, inc=False):
        if self.which == eng:
            inst = fn(self.h)
            if inc:
                inst.then_inc(self.sems[eng], 1)
        if inc:
            self.n[eng] += 1
            return self.n[eng]
        return None

    def wait(self, eng, sem_name, ticket):
        if ticket is None or (isinstance(ticket, int) and ticket <= 0):
            return
        if self.which == eng:
            mult = 16 if sem_name in ("g", "s", "a", "v") else 1
            self.h.wait_ge(self.sems[sem_name], mult * ticket)


def run_phase(nc, sems, base, phase_fn):
    finals = {}
    with nc.Block() as block:
        def mk(which):
            def body(handle):
                E = Emitter(which, handle, sems, base)
                phase_fn(E)
                finals[which] = dict(E.n)
            return body
        block.tensor(mk("tensor"))
        block.vector(mk("vector"))
        block.scalar(mk("scalar"))
        block.gpsimd(mk("gpsimd"))
        block.sync(mk("sync"))
    f0 = finals["tensor"]
    for k, v in finals.items():
        assert v == f0, f"ticket divergence {k}: {v} vs {f0}"
    return f0


def build_program(max_phase=99):
    nc = bass.Bass(trn_type="TRN2", target_bir_lowering=False)

    x_in = nc.dram_tensor("x", [S, D], F32, kind="ExternalInput").ap()
    wq = nc.dram_tensor("wq", [D, D], F32, kind="ExternalInput").ap()
    wk = nc.dram_tensor("wk", [512, D], F32, kind="ExternalInput").ap()
    wv = nc.dram_tensor("wv", [512, D], F32, kind="ExternalInput").ap()
    wo = nc.dram_tensor("wo", [D, D], F32, kind="ExternalInput").ap()
    wgu = nc.dram_tensor("wgu", [2 * H, D], F32, kind="ExternalInput").ap()
    wp = nc.dram_tensor("wp", [D, H], F32, kind="ExternalInput").ap()
    c2t = nc.dram_tensor("c2t", [64, S], F32, kind="ExternalInput").ap()
    s2t = nc.dram_tensor("s2t", [64, S], F32, kind="ExternalInput").ap()
    tri_in = nc.dram_tensor("tri", [128, 128], BF16, kind="ExternalInput").ap()
    onesf_in = nc.dram_tensor("onesf", [1, 128], F32, kind="ExternalInput").ap()
    halvesf_in = nc.dram_tensor("halvesf", [1, 128], F32, kind="ExternalInput").ap()
    onesbf_in = nc.dram_tensor("onesbf", [128, 1], BF16, kind="ExternalInput").ap()
    idf_in = nc.dram_tensor("idf", [128, 128], F32, kind="ExternalInput").ap()
    epsc_in = nc.dram_tensor("epsc", [1, 2], F32, kind="ExternalInput").ap()
    out = nc.dram_tensor("out", [S, D], F32, kind="ExternalOutput").ap()

    x0f = nc.dram_tensor("x0f", [NDC, NTT, 128, TT], F32, kind="Internal").ap()
    x0fb = nc.dram_tensor("x0fb", [NDC, NTT, 128, TT], BF16, kind="Internal").ap()
    x1f = nc.dram_tensor("x1f", [NDC, NTT, 128, TT], F32, kind="Internal").ap()
    x2f = nc.dram_tensor("x2f", [NDC, NTT, 128, TT], F32, kind="Internal").ap()
    hbuf = nc.dram_tensor("hbuf", [H, S], BF16, kind="Internal").ap()
    wgu_q = nc.dram_tensor("wgu_q", [2 * H, D], BF16, kind="Internal").ap()
    wp_q = nc.dram_tensor("wp_q", [D, H], BF16, kind="Internal").ap()
    wq_s = nc.dram_tensor("wq_s", [D, D], BF16, kind="Internal").ap()
    wk_s = nc.dram_tensor("wk_s", [512, D], BF16, kind="Internal").ap()
    wv_s = nc.dram_tensor("wv_s", [512, D], BF16, kind="Internal").ap()
    wo_s = nc.dram_tensor("wo_s", [D, D], BF16, kind="Internal").ap()

    gctx = ExitStack()
    gsb = lambda name, shape, dt: gctx.enter_context(nc.sbuf_tensor(name, shape, dt))
    idf = gsb("idf_s", [128, 128], F32)
    onesf = gsb("onesf_s", [1, 128], F32)
    halvesf = gsb("halvesf_s", [1, 128], F32)
    onesbf = gsb("onesbf_s", [128, 1], BF16)
    tri = gsb("tri_s", [128, 128], BF16)
    epsc = gsb("epsc_s", [1, 2], F32)

    sems = {}
    for name in ("g", "s", "a", "v", "tensor", "vector", "scalar", "gpsimd"):
        sems[name] = gctx.enter_context(nc.semaphore(f"sem_{name}"))
    base = [{k: 0 for k in ("g", "s", "a", "v", "tensor", "vector", "scalar", "gpsimd")}]

    class QuantPump:
        def __init__(self, E, Qsets):
            self.E = E
            self.Q = Qsets
            self.done_t = {}
            self.wr = [None, None]
            self.loaded = {}
            self.u = 0
            self.NU = 192

        def _params(self, u):
            if u < 128:
                half, idx = u % 2, u // 2
                r0 = (H if half else 0) + 128 * idx
                return (wgu[r0:r0 + 128, :], wgu_q[r0:r0 + 128, :])
            v = u - 128
            q, slab = v // 16, v % 16
            return (wp[128 * slab:128 * slab + 128, 2048 * q:2048 * q + 2048],
                    wp_q[128 * slab:128 * slab + 128, 2048 * q:2048 * q + 2048])

        def _emit_load(self, u):
            if u >= self.NU or u in self.loaded:
                return
            E = self.E
            s = u % 2
            qb = self.Q[s][0]
            src, _ = self._params(u)
            E.wait("gpsimd", "a", self.wr[s])
            self.loaded[u] = E.dma("v", out=qb[:], in_=src)

        def _emit_unit(self, u):
            E = self.E
            s = u % 2
            qb, ta, tb, qsums, qsbf, qshalf, qnshalf = self.Q[s]
            src, dst = self._params(u)
            self._emit_load(u)
            t_in = self.loaded[u]
            self._emit_load(u + 1)
            E.wait("vector", "v", t_in)
            t_red = E.op("vector", lambda v: v.tensor_reduce(
                out=qsums[:, 0:32], in_=qb[:].rearrange("p (g k) -> p g k", k=64),
                axis=AX.X, op=OP.add, apply_absolute_value=True), inc=True)
            E.wait("scalar", "vector", t_red)
            t_mul = E.op("scalar", lambda sc: sc.mul(qsbf[:, 0:32], qsums[:, 0:32], 1.0 / 64.0), inc=True)
            E.wait("vector", "scalar", t_mul)
            t = E.op("vector", lambda v: v.tensor_scalar_max(qsbf[:, 0:32], qsbf[:, 0:32], 1e-8), inc=True)
            E.wait("vector", "vector", t)
            E.op("vector", lambda v: v.tensor_scalar_mul(qshalf[:, 0:32], qsbf[:, 0:32], 0.5), inc=True)
            t = E.op("vector", lambda v: v.tensor_scalar_mul(qnshalf[:, 0:32], qsbf[:, 0:32], -0.5), inc=True)
            E.wait("vector", "vector", t)
            E.op("vector", lambda v: v.tensor_tensor(
                out=ta[:].rearrange("p (g k) -> p g k", k=64),
                in0=qb[:].rearrange("p (g k) -> p g k", k=64),
                in1=qshalf[:, 0:32].unsqueeze(2).broadcast_to([128, 32, 64]), op=OP.is_gt), inc=True)
            t = E.op("vector", lambda v: v.tensor_tensor(
                out=tb[:].rearrange("p (g k) -> p g k", k=64),
                in0=qb[:].rearrange("p (g k) -> p g k", k=64),
                in1=qnshalf[:, 0:32].unsqueeze(2).broadcast_to([128, 32, 64]), op=OP.is_lt), inc=True)
            E.wait("vector", "vector", t)
            t = E.op("vector", lambda v: v.tensor_sub(qb[:], ta[:], tb[:]), inc=True)
            E.wait("vector", "vector", t)
            t_q = E.op("vector", lambda v: v.tensor_tensor(
                out=ta[:].rearrange("p (g k) -> p g k", k=64),
                in0=qb[:].rearrange("p (g k) -> p g k", k=64),
                in1=qsbf[:, 0:32].unsqueeze(2).broadcast_to([128, 32, 64]), op=OP.mult), inc=True)
            E.wait("scalar", "vector", t_q)
            t_wr = E.dma("a", out=dst, in_=ta[:])
            self.wr[s] = t_wr
            self.done_t[u] = t_wr

        def units(self, n):
            for _ in range(n):
                if self.u >= self.NU:
                    return
                self._emit_unit(self.u)
                self.u += 1

        def drain(self):
            self.units(self.NU - self.u)

    wprep = {}

    def emit_staging(E):
        wqv = wq.rearrange("(j2 head t pair par) d -> j2 t par head pair d",
                           j2=4, head=2, t=4, pair=32, par=2)
        for s_ in range(16):
            tt_ = None
            for blkk in range(4):
                par_, head_ = blkk // 2, blkk % 2
                tt_ = E.dma("g", out=wq_s[128 * s_ + 32 * blkk:128 * s_ + 32 * blkk + 32, :],
                            in_=wqv[s_ // 4, s_ % 4, par_, head_])
            wprep[("q", s_)] = tt_
        wkv = wk.rearrange("(j2 head pair par) d -> j2 par head pair d",
                           j2=4, head=2, pair=32, par=2)
        for s_ in range(4):
            tt_ = None
            for blkk in range(4):
                par_, head_ = blkk // 2, blkk % 2
                tt_ = E.dma("g", out=wk_s[128 * s_ + 32 * blkk:128 * s_ + 32 * blkk + 32, :],
                            in_=wkv[s_, par_, head_])
            wprep[("k", s_)] = tt_
        wprep[("v", 0)] = E.dma("g", out=wv_s, in_=wv)
        wprep[("o", 0)] = E.dma("g", out=wo_s, in_=wo)

    def emit_prologue(E, x_src, z_dst, B, nbuf=2, ph0=None, pump=None, lq="g",
                      after_tt=None, sq1=False):
        (xcache, xsqb, row_e2, row_msq, row_var, row_ln, zt_f32,
         stats_m, stats_r, ps_st0, ps_st1, ps_bcm, ps_bcr) = B
        sqp = 1 if sq1 else 2
        le = {"g": "gpsimd", "s": "sync"}[lq]
        z_done = {}
        zop = {}
        sq_of = {}
        srow_done = {}
        bc_done = {}
        for tt in range(NTT):
            xc_off = (tt % nbuf) * (NDC * TT)
            st1_of_dc = {}
            for dc in range(NDC):
                if pump is not None and dc % 2 == 1:
                    pump.units(1)
                E.wait(le, "vector", zop.get((tt - nbuf, dc)))
                E.wait(le, "scalar", sq_of.get((tt - nbuf, dc)))
                if ph0 is not None:
                    E.wait(le, "s", ph0[(dc, tt)])
                t_ld = E.dma(lq, out=xcache[:, xc_off + TT * dc: xc_off + TT * dc + TT],
                             in_=x_src[dc, tt])
                E.wait("scalar", lq, t_ld)
                E.wait("scalar", "tensor", st1_of_dc.get(dc - sqp))
                t_sq = E.op("scalar", lambda sc, dc=dc: sc.activation(
                    out=xsqb[:, TT * (dc % sqp):TT * (dc % sqp) + TT],
                    in_=xcache[:, xc_off + TT * dc: xc_off + TT * dc + TT],
                    func=AF.Square), inc=True)
                sq_of[(tt, dc)] = t_sq
                E.wait("tensor", lq, t_ld)
                E.wait("tensor", "scalar", t_sq)
                if dc == 0:
                    E.wait("tensor", "scalar", srow_done.get(tt - 1))
                E.op("tensor", lambda t, dc=dc: t.matmul(
                    ps_st0[:], onesbf[:],
                    xcache[:, xc_off + TT * dc: xc_off + TT * dc + TT],
                    start=(dc == 0), stop=(dc == NDC - 1)))
                t_mm = E.op("tensor", lambda t, dc=dc: t.matmul(
                    ps_st1[:], onesbf[:],
                    xsqb[:, TT * (dc % sqp):TT * (dc % sqp) + TT],
                    start=(dc == 0), stop=(dc == NDC - 1)), inc=True)
                st1_of_dc[dc] = t_mm
            E.wait("scalar", "tensor", t_mm)
            E.wait("scalar", "tensor", bc_done.get(tt - 1))
            E.op("scalar", lambda sc, tt=tt: sc.mul(stats_m[:, 0:TT], ps_st0[:], 1.0 / D))
            t_e2 = E.op("scalar", lambda sc: sc.mul(row_e2[:], ps_st1[:], 1.0 / D), inc=True)
            t_msq = E.op("scalar", lambda sc, tt=tt: sc.activation(
                out=row_msq[:], in_=stats_m[:, 0:TT], func=AF.Square), inc=True)
            srow_done[tt] = t_msq
            E.wait("vector", "scalar", t_msq)
            t_var = E.op("vector", lambda v: v.tensor_sub(row_var[:], row_e2[:], row_msq[:]), inc=True)
            E.wait("scalar", "vector", t_var)
            t_ln = E.op("scalar", lambda sc: sc.activation(
                out=row_ln[:], in_=row_var[:], func=AF.Ln, bias=epsc[0:1, 0:1]), inc=True)
            E.wait("scalar", "scalar", t_ln)
            t_rs = E.op("scalar", lambda sc, tt=tt: sc.activation(
                out=stats_r[:, 0:TT], in_=row_ln[:], func=AF.Exp, scale=-0.5), inc=True)
            E.wait("tensor", "scalar", t_rs)
            E.wait("tensor", "vector", z_done.get(tt - 1))
            E.op("tensor", lambda t, tt=tt: t.matmul(
                ps_bcm[:], onesf[:], stats_m[:, 0:TT], start=True, stop=True))
            t_bc = E.op("tensor", lambda t, tt=tt: t.matmul(
                ps_bcr[:], onesf[:], stats_r[:, 0:TT], start=True, stop=True), inc=True)
            bc_done[tt] = t_bc
            E.wait("vector", "tensor", t_bc)
            tz = None
            for dc in range(NDC):
                t_sub = E.op("vector", lambda v, dc=dc: v.tensor_sub(
                    zt_f32[:, 0:TT],
                    xcache[:, xc_off + TT * dc: xc_off + TT * dc + TT], ps_bcm[:]), inc=True)
                E.wait("vector", "vector", t_sub)
                tz = E.op("vector", lambda v, dc=dc, tt=tt: v.tensor_mul(
                    z_dst[:, S * dc + TT * tt: S * dc + TT * tt + TT],
                    zt_f32[:, 0:TT], ps_bcr[:]), inc=True)
                zop[(tt, dc)] = tz
            z_done[tt] = tz
            if after_tt is not None:
                after_tt(tt, tz)
        return z_done[NTT - 1]

    def prologue_phase(x_src, z_dst, tag, nbuf=2):
        with nc.sbuf_tensor("xc" + tag, [128, nbuf * NDC * TT], BF16) as xcache, \
             nc.sbuf_tensor("xq" + tag, [128, 2 * TT], BF16) as xsqb, \
             nc.sbuf_tensor("re" + tag, [1, TT], F32) as row_e2, \
             nc.sbuf_tensor("rm" + tag, [1, TT], F32) as row_msq, \
             nc.sbuf_tensor("rv" + tag, [1, TT], F32) as row_var, \
             nc.sbuf_tensor("rl" + tag, [1, TT], F32) as row_ln, \
             nc.sbuf_tensor("zt" + tag, [128, TT], F32) as zt_f32, \
             nc.sbuf_tensor("sm" + tag, [1, TT], F32) as stats_m, \
             nc.sbuf_tensor("sr" + tag, [1, TT], F32) as stats_r, \
             nc.psum_tensor("q0" + tag, [1, TT], F32) as ps_st0, \
             nc.psum_tensor("q1" + tag, [1, TT], F32) as ps_st1, \
             nc.psum_tensor("qm" + tag, [128, TT], F32) as ps_bcm, \
             nc.psum_tensor("qr" + tag, [128, TT], F32) as ps_bcr:
            B = (xcache, xsqb, row_e2, row_msq, row_var, row_ln, zt_f32,
                 stats_m, stats_r, ps_st0, ps_st1, ps_bcm, ps_bcr)
            base[0] = run_phase(nc, sems, base[0],
                                lambda E: emit_prologue(E, x_src, z_dst, B, nbuf))

    def emit_gu(E, z, wgT, wuT, htile, sgb, pg, pu, pump=None, tread_dep=None,
                z_ticket=None):
        ev_done = {}
        mm_grp_last = {}
        h_wr = {}
        tread_of = {}
        fill_i = 0

        def emit_treads(grp):
            par = grp % 2
            t_need = tread_dep(grp) if tread_dep is not None else None
            tread_last = None
            for dc in range(NDC):
                E.wait("sync", "a", t_need)
                E.wait("sync", "tensor", mm_grp_last.get(grp - 2))
                E.tdma("s", out=wgT[:, par * 16 * TT + TT * dc: par * 16 * TT + TT * dc + TT],
                       in_=wgu_q[512 * grp:512 * grp + 512, 128 * dc:128 * dc + 128])
                tread_last = E.tdma(
                    "s", out=wuT[:, par * 16 * TT + TT * dc: par * 16 * TT + TT * dc + TT],
                    in_=wgu_q[H + 512 * grp:H + 512 * grp + 512, 128 * dc:128 * dc + 128])
            tread_of[grp] = tread_last

        for grp in range(16):
            par = grp % 2
            if grp == 0:
                emit_treads(0)
            if grp + 1 < 16:
                emit_treads(grp + 1)
            tread_last = tread_of[grp]
            for sl in range(4):
                i = 4 * grp + sl
                for ttp in range(2):
                    fp = fill_i % 2
                    tt0, tt1 = 2 * ttp, 2 * ttp + 1
                    E.wait("tensor", "s", tread_last)
                    if fill_i == 0 and z_ticket is not None:
                        E.wait("tensor", "vector", z_ticket)
                    prev = ev_done.get(fill_i - 2)
                    if prev is not None:
                        E.wait("tensor", "scalar", prev[0])
                        E.wait("tensor", "vector", prev[1])
                    t_mm = None
                    for dc in range(NDC):
                        gcol = par * 16 * TT + TT * dc + 128 * sl
                        st, sp = (dc == 0), (dc == NDC - 1)
                        E.op("tensor", lambda t, gcol=gcol, st=st, sp=sp, dc=dc, tt0=tt0, fp=fp: t.matmul(
                            pg[2 * fp][:], wgT[:, gcol:gcol + 128],
                            z[:, S * dc + TT * tt0: S * dc + TT * tt0 + TT], start=st, stop=sp))
                        E.op("tensor", lambda t, gcol=gcol, st=st, sp=sp, dc=dc, tt1=tt1, fp=fp: t.matmul(
                            pg[2 * fp + 1][:], wgT[:, gcol:gcol + 128],
                            z[:, S * dc + TT * tt1: S * dc + TT * tt1 + TT], start=st, stop=sp))
                        E.op("tensor", lambda t, gcol=gcol, st=st, sp=sp, dc=dc, tt0=tt0, fp=fp: t.matmul(
                            pu[2 * fp][:], wuT[:, gcol:gcol + 128],
                            z[:, S * dc + TT * tt0: S * dc + TT * tt0 + TT], start=st, stop=sp))
                        t_mm = E.op("tensor", lambda t, gcol=gcol, st=st, sp=sp, dc=dc, tt1=tt1, fp=fp: t.matmul(
                            pu[2 * fp + 1][:], wuT[:, gcol:gcol + 128],
                            z[:, S * dc + TT * tt1: S * dc + TT * tt1 + TT], start=st, stop=sp),
                            inc=sp)
                    if sl == 3 and ttp == 1:
                        mm_grp_last[grp] = t_mm
                    E.wait("scalar", "tensor", t_mm)
                    prev1 = ev_done.get(fill_i - 1)
                    if prev1 is not None:
                        E.wait("scalar", "vector", prev1[1])
                    E.op("scalar", lambda sc, fp=fp: sc.activation(
                        out=sgb[:, 0:TT], in_=pg[2 * fp][:], func=AF.Silu))
                    t_sg = E.op("scalar", lambda sc, fp=fp: sc.activation(
                        out=sgb[:, TT:2 * TT], in_=pg[2 * fp + 1][:], func=AF.Silu), inc=True)
                    E.wait("vector", "scalar", t_sg)
                    E.wait("vector", "tensor", t_mm)
                    E.wait("vector", "s", h_wr.get(i % 2) if i >= 2 else None)
                    hpar = i % 2
                    E.op("vector", lambda v, fp=fp, hpar=hpar, tt0=tt0: v.tensor_mul(
                        htile[:, S * hpar + TT * tt0: S * hpar + TT * tt0 + TT],
                        sgb[:, 0:TT], pu[2 * fp][:]), inc=True)
                    t_hm = E.op("vector", lambda v, fp=fp, hpar=hpar, tt1=tt1: v.tensor_mul(
                        htile[:, S * hpar + TT * tt1: S * hpar + TT * tt1 + TT],
                        sgb[:, TT:2 * TT], pu[2 * fp + 1][:]), inc=True)
                    ev_done[fill_i] = (t_sg, t_hm)
                    fill_i += 1
                    if pump is not None:
                        pump.units(1)
                E.wait("sync", "vector", ev_done[fill_i - 1][1])
                h_wr[i % 2] = E.dma("s", out=hbuf[128 * i:128 * i + 128, :],
                                    in_=htile[:, S * (i % 2):S * (i % 2) + S])
                if pump is not None and sl % 2 == 1:
                    pump.units(1)
        if pump is not None:
            pump.drain()

    def ffn_gu(x_src, z, tag):
        with ExitStack() as gu:
            esb = lambda n, sh, dt: gu.enter_context(nc.sbuf_tensor(n + tag, sh, dt))
            eps = lambda n, sh: gu.enter_context(nc.psum_tensor(n + tag, sh, F32))
            wgT = esb("wgT", [128, 2 * 16 * TT], BF16)
            wuT = esb("wuT", [128, 2 * 16 * TT], BF16)
            htile = esb("htile", [128, 2 * S], BF16)
            sgb = esb("sgb", [128, 2 * TT], F32)
            pg = [eps("pg0", [128, TT]), eps("pg1", [128, TT]),
                  eps("pg2", [128, TT]), eps("pg3", [128, TT])]
            pu = [eps("pu0", [128, TT]), eps("pu1", [128, TT]),
                  eps("pu2", [128, TT]), eps("pu3", [128, TT])]

            def gu_phase(E):
                emit_gu(E, z, wgT, wuT, htile, sgb, pg, pu)
                E.wait("gpsimd", "g", E.n["g"])
                E.wait("sync", "s", E.n["s"])

            base[0] = run_phase(nc, sems, base[0], gu_phase)

    def ffn_proj(x_src, x_dst, final, tag, x_tm=None):
        with ExitStack() as pj:
            esb = lambda n, sh, dt: pj.enter_context(nc.sbuf_tensor(n + tag, sh, dt))
            eps = lambda n, sh: pj.enter_context(nc.psum_tensor(n + tag, sh, F32))
            acc = esb("acc", [128, NDC * 1024], F32)
            hgrp = esb("hgrp", [128, 2 * 8 * 1024], BF16)
            wpT = esb("wpT", [128, 2 * 8 * D], BF16)
            hsqb = esb("hsqb", [128, 2 * 1024], BF16)
            invrow = esb("invrow", [1, 1024], F32)
            rowl = esb("rowl", [1, TT], F32)
            tmul = esb("tmul", [128, 2 * TT], F32)
            xprev = esb("xprev", [128, 2 * TT], F32)
            obuf = esb("obuf", [128, 4 * 128], F32) if final else None
            pp = [eps("pp0", [128, TT]), eps("pp1", [128, TT]),
                  eps("pp2", [128, TT]), eps("pp3", [128, TT])]
            ps_ssq = [eps("pssq0", [1, TT]), eps("pssq1", [1, TT])]
            ps_bc = eps("pbc", [128, TT])

            def proj_phase(E):
                ep_prev_v = None
                ep_inv_done = None
                ssq_prev = None
                obuf_wr = {}
                ot_ev = {}
                for th in range(2):
                    flush_t = {}
                    pe_last = {}
                    sqd = {}
                    ssq_mm = None
                    for hg in range(8):
                        gpar = hg % 2
                        wload_last = None
                        for hcl in range(8):
                            hc = 8 * hg + hcl
                            E.wait("sync", "tensor", pe_last.get((th, hg - 2)))
                            E.wait("sync", "scalar", sqd.get((th, hg - 2)))
                            E.dma("s", out=hgrp[:, (gpar * 8 + hcl) * 1024:(gpar * 8 + hcl) * 1024 + 1024],
                                  in_=hbuf[128 * hc:128 * hc + 128, 1024 * th:1024 * th + 1024])
                            wload_last = E.tdma(
                                "s", out=wpT[:, (gpar * 8 + hcl) * D:(gpar * 8 + hcl) * D + D],
                                in_=wp_q[:, 128 * hc:128 * hc + 128])
                        for hcl in range(8):
                            hld = wload_last - (7 - hcl) * 2 - 1
                            E.wait("scalar", "s", hld)
                            E.wait("scalar", "tensor", ssq_prev)
                            t_sq = E.op("scalar", lambda sc, gpar=gpar, hcl=hcl: sc.activation(
                                out=hsqb[:, (gpar) * 1024:(gpar) * 1024 + 1024],
                                in_=hgrp[:, (gpar * 8 + hcl) * 1024:(gpar * 8 + hcl) * 1024 + 1024],
                                func=AF.Square), inc=True)
                            sqd[(th, hg)] = t_sq
                            E.wait("tensor", "scalar", t_sq)
                            if th == 1 and hg == 0 and hcl == 0:
                                E.wait("tensor", "scalar", ep_inv_done)
                            for tq in range(2):
                                st = (hg == 0 and hcl == 0)
                                sp = (hg == 7 and hcl == 7)
                                t_s = E.op("tensor", lambda t, tq=tq, st=st, sp=sp, gpar=gpar, hcl=hcl: t.matmul(
                                    ps_ssq[tq][:], onesbf[:],
                                    hsqb[:, gpar * 1024 + TT * tq: gpar * 1024 + TT * tq + TT],
                                    start=st, stop=sp), inc=(tq == 1))
                                if tq == 1:
                                    ssq_prev = t_s
                                if sp:
                                    ssq_mm = t_s
                        E.wait("tensor", "s", wload_last)
                        for d16 in range(NDC):
                            prev = flush_t.get((hg, d16 - 2)) if d16 >= 2 else (
                                flush_t.get((hg - 1, NDC - 2 + d16)) if hg >= 1 else None)
                            E.wait("tensor", "vector", prev)
                            t_f = None
                            for hcl in range(8):
                                wcol = (gpar * 8 + hcl) * D + 128 * d16
                                st, sp = (hcl == 0), (hcl == 7)
                                E.op("tensor", lambda t, wcol=wcol, st=st, sp=sp, gpar=gpar, hcl=hcl, d16=d16: t.matmul(
                                    pp[2 * (d16 % 2)][:], wpT[:, wcol:wcol + 128],
                                    hgrp[:, (gpar * 8 + hcl) * 1024:(gpar * 8 + hcl) * 1024 + TT],
                                    start=st, stop=sp))
                                t_f = E.op("tensor", lambda t, wcol=wcol, st=st, sp=sp, gpar=gpar, hcl=hcl, d16=d16: t.matmul(
                                    pp[2 * (d16 % 2) + 1][:], wpT[:, wcol:wcol + 128],
                                    hgrp[:, (gpar * 8 + hcl) * 1024 + TT:(gpar * 8 + hcl) * 1024 + 2 * TT],
                                    start=st, stop=sp), inc=sp)
                            E.wait("vector", "tensor", t_f)
                            for tq in range(2):
                                col = 1024 * d16 + TT * tq
                                if hg == 0:
                                    t_fl = E.op("vector", lambda v, tq=tq, col=col, d16=d16: v.tensor_copy(
                                        out=acc[:, col:col + TT], in_=pp[2 * (d16 % 2) + tq][:]), inc=True)
                                else:
                                    t_fl = E.op("vector", lambda v, tq=tq, col=col, d16=d16: v.tensor_add(
                                        acc[:, col:col + TT], acc[:, col:col + TT],
                                        pp[2 * (d16 % 2) + tq][:]), inc=True)
                            flush_t[(hg, d16)] = t_fl
                            if d16 == NDC - 1:
                                pe_last[(th, hg)] = t_f
                    last_flush = flush_t[(7, NDC - 1)]
                    E.wait("scalar", "tensor", ssq_mm)
                    t_inv = None
                    for tq in range(2):
                        t_ln = E.op("scalar", lambda sc, tq=tq: sc.activation(
                            out=rowl[:], in_=ps_ssq[tq][:], func=AF.Ln, scale=1.0 / H, bias=epsc[0:1, 1:2]), inc=True)
                        E.wait("scalar", "scalar", t_ln)
                        t_inv = E.op("scalar", lambda sc, tq=tq: sc.activation(
                            out=invrow[:, TT * tq:TT * tq + TT], in_=rowl[:], func=AF.Exp, scale=-0.5), inc=True)
                    ep_inv_done = t_inv
                    for tq in range(2):
                        tt = 2 * th + tq
                        E.wait("tensor", "scalar", t_inv)
                        E.wait("tensor", "vector", ep_prev_v)
                        E.wait("tensor", "vector", last_flush)
                        t_bc = E.op("tensor", lambda t, tq=tq: t.matmul(
                            ps_bc[:], halvesf[:], invrow[:, TT * tq:TT * tq + TT],
                            start=True, stop=True), inc=True)
                        for d16 in range(NDC):
                            dpar = d16 % 2
                            E.wait("sync", "vector", ep_prev_v)
                            if x_tm is not None:
                                t_xl = E.tdma("s", out=xprev[:, TT * dpar:TT * dpar + TT],
                                              in_=x_tm[512 * tt:512 * tt + 512, 128 * d16:128 * d16 + 128])
                            else:
                                t_xl = E.dma("s", out=xprev[:, TT * dpar:TT * dpar + TT], in_=x_src[d16, tt])
                            E.wait("vector", "tensor", t_bc)
                            E.wait("vector", "s", t_xl)
                            E.wait("vector", "vector", last_flush)
                            E.op("vector", lambda v, d16=d16, tq=tq, dpar=dpar: v.tensor_mul(
                                tmul[:, TT * dpar:TT * dpar + TT],
                                acc[:, 1024 * d16 + TT * tq:1024 * d16 + TT * tq + TT], ps_bc[:]), inc=True)
                            t_v = E.op("vector", lambda v, dpar=dpar: v.tensor_add(
                                tmul[:, TT * dpar:TT * dpar + TT], tmul[:, TT * dpar:TT * dpar + TT],
                                xprev[:, TT * dpar:TT * dpar + TT]), inc=True)
                            ep_prev_v = t_v
                            if not final:
                                E.wait("sync", "vector", t_v)
                                E.dma("s", out=x_dst[d16, tt], in_=tmul[:, TT * dpar:TT * dpar + TT])
                            else:
                                for k in range(4):
                                    tpar = k % 2
                                    E.wait("tensor", "vector", t_v)
                                    E.wait("tensor", "scalar", ot_ev.get(tpar))
                                    t_tp = E.op("tensor", lambda t, dpar=dpar, k=k, tpar=tpar: t.transpose(
                                        pp[tpar][:, 0:128],
                                        tmul[:, TT * dpar + 128 * k:TT * dpar + 128 * k + 128],
                                        idf[:]), inc=True)
                                    E.wait("scalar", "tensor", t_tp)
                                    E.wait("scalar", "s", obuf_wr.get(k))
                                    t_ev = E.op("scalar", lambda sc, tpar=tpar, k=k: sc.copy(
                                        out=obuf[:, 128 * k:128 * k + 128], in_=pp[tpar][:, 0:128]), inc=True)
                                    ot_ev[tpar] = t_ev
                                    E.wait("sync", "scalar", t_ev)
                                    tc_glob = 4 * tt + k
                                    obuf_wr[k] = E.dma(
                                        "s", out=out[128 * tc_glob:128 * tc_glob + 128,
                                                     128 * d16:128 * d16 + 128],
                                        in_=obuf[:, 128 * k:128 * k + 128])
                E.wait("sync", "s", E.n["s"])

            base[0] = run_phase(nc, sems, base[0], proj_phase)

    if max_phase >= 1:
        with ExitStack() as ba:
            asb = lambda n, sh, dt: ba.enter_context(nc.sbuf_tensor(n, sh, dt))
            aps = lambda n: ba.enter_context(nc.psum_tensor(n, [128, TT], F32))
            xin = asb("xinA", [128, D], F32)
            xtp = asb("xtpA", [128, 4 * 128], F32)
            xtpb = asb("xtpbA", [128, 4 * 128], BF16)
            z1 = asb("z_f1", [128, NDC * S], BF16)
            xcache = asb("xcf1", [128, NDC * TT], BF16)
            xsqb = asb("xqf1", [128, 2 * TT], BF16)
            row_e2 = asb("ref1", [1, TT], F32)
            row_msq = asb("rmf1", [1, TT], F32)
            row_var = asb("rvf1", [1, TT], F32)
            row_ln = asb("rlf1", [1, TT], F32)
            zt_f32 = asb("ztf1", [128, TT], F32)
            stats_m = asb("smf1", [1, TT], F32)
            stats_r = asb("srf1", [1, TT], F32)
            Qsets = []
            for qi in range(2):
                qs = str(qi)
                Qsets.append((asb("qb" + qs, [128, 2048], BF16), asb("ta" + qs, [128, 2048], BF16),
                              asb("tb" + qs, [128, 2048], BF16), asb("qsums" + qs, [128, 32], F32),
                              asb("qsbf" + qs, [128, 32], BF16), asb("qshalf" + qs, [128, 32], BF16),
                              asb("qnshalf" + qs, [128, 32], BF16)))
            wgT = asb("wgTf1", [128, 2 * 16 * TT], BF16)
            wuT = asb("wuTf1", [128, 2 * 16 * TT], BF16)
            htile = asb("htf1", [128, 2 * S], BF16)
            sgb = asb("sgf1", [128, 2 * TT], BF16)
            pg = [aps("pgA0"), aps("pgA1"), aps("pgA2"), aps("pgA3")]
            pu = [aps("puA0"), aps("puA1"), aps("puA2"), aps("puA3")]
            p0ps = [pg[0][:, 0:128], pg[1][:, 0:128]]
            B1 = (xcache, xsqb, row_e2, row_msq, row_var, row_ln, zt_f32,
                  stats_m, stats_r, pu[0][0:1, :], pu[1][0:1, :], pg[2], pg[3])

            def blockA(E):
                pump = QuantPump(E, Qsets)
                E.dma("s", out=idf[:], in_=idf_in)
                E.dma("s", out=onesf[:], in_=onesf_in)
                E.dma("s", out=halvesf[:], in_=halvesf_in)
                E.dma("s", out=onesbf[:], in_=onesbf_in)
                E.dma("s", out=epsc[:], in_=epsc_in)
                t_c = E.dma("s", out=tri[:], in_=tri_in)
                E.wait("tensor", "s", t_c)
                ph0 = {}
                tp_i = 0
                last_ev = {}
                lastA = {}
                lastS = {}
                half_tp = {}
                t_ld = {}
                for tc in range(16):
                    E.wait("sync", "tensor", half_tp.get(tc - 1))
                    t_ld = E.dma("s", out=xin[:, 0:D],
                                 in_=x_in[128 * tc:128 * tc + 128, :])
                    for dc in range(NDC):
                        par = tp_i % 2
                        slot = tp_i % 4
                        if dc == 0:
                            E.wait("tensor", "s", t_ld)
                        E.wait("tensor", "scalar", last_ev.get(par))
                        t_mm = E.op("tensor", lambda t, dc=dc, par=par: t.transpose(
                            p0ps[par][:],
                            xin[:, 128 * dc:128 * dc + 128],
                            idf[:]), inc=True)
                        if dc % 8 == 7:
                            half_tp[tc] = t_mm
                        E.wait("scalar", "tensor", t_mm)
                        E.wait("scalar", "s", lastS.get(slot))
                        E.op("scalar", lambda sc, par=par, slot=slot: sc.copy(
                            out=xtp[:, 128 * slot:128 * slot + 128], in_=p0ps[par][:]))
                        t_ev = E.op("scalar", lambda sc, par=par, slot=slot: sc.copy(
                            out=xtpb[:, 128 * slot:128 * slot + 128], in_=p0ps[par][:]), inc=True)
                        last_ev[par] = t_ev
                        E.wait("sync", "scalar", t_ev)
                        E.dma("s", out=x0f[dc, tc // 4, :, 128 * (tc % 4):128 * (tc % 4) + 128],
                              in_=xtp[:, 128 * slot:128 * slot + 128])
                        t_wr = E.dma("s", out=x0fb[dc, tc // 4, :, 128 * (tc % 4):128 * (tc % 4) + 128],
                                     in_=xtpb[:, 128 * slot:128 * slot + 128])
                        lastS[slot] = t_wr
                        if tc % 4 == 3:
                            ph0[(dc, tc // 4)] = t_wr
                        tp_i += 1
                    pump.units(1)
                zt3 = emit_prologue(E, x0fb, z1[:], B1, nbuf=1, ph0=ph0, pump=pump,
                                    lq="s")
                emit_gu(E, z1[:], wgT, wuT, htile, sgb, pg, pu, pump=pump,
                        tread_dep=lambda g: pump.done_t[8 * g + 7], z_ticket=zt3)
                emit_staging(E)
                E.wait("sync", "s", E.n["s"])
                E.wait("scalar", "a", E.n["a"])

            base[0] = run_phase(nc, sems, base[0], blockA)
        if max_phase >= 3:
            ffn_proj(x0f, x1f, final=False, tag="f1")

    if max_phase < 4:
        gctx.close()
        return nc
    with nc.sbuf_tensor("qf", [128, 16 * S], BF16) as qf, \
         nc.sbuf_tensor("kf", [128, 4 * S], BF16) as kf, \
         nc.sbuf_tensor("vtm", [128, 16 * 520], BF16) as vtm:

        with nc.sbuf_tensor("z_at", [128, NDC * S], BF16) as z2:
            with nc.sbuf_tensor("xcat", [128, NDC * TT], BF16) as xc_at, \
                 nc.sbuf_tensor("xqat", [128, TT], BF16) as xsq_at, \
                 nc.sbuf_tensor("reat", [1, TT], F32) as re_at, \
                 nc.sbuf_tensor("rmat", [1, TT], F32) as rm_at, \
                 nc.sbuf_tensor("rvat", [1, TT], F32) as rv_at, \
                 nc.sbuf_tensor("ztat", [128, TT], F32) as zt_at, \
                 nc.sbuf_tensor("smat", [33, TT], F32) as sm_at, \
                 nc.sbuf_tensor("srat", [1, TT], F32) as sr_at, \
                 nc.sbuf_tensor("wvTs", [128, 16 * TT], BF16) as wvT, \
                 nc.psum_tensor("q0at", [1, TT], F32) as st0_at, \
                 nc.psum_tensor("q1at", [1, TT], F32) as st1_at, \
                 nc.psum_tensor("qmat", [128, TT], F32) as bcm_at, \
                 nc.psum_tensor("qrat", [128, TT], F32) as bcr_at, \
                 nc.psum_tensor("pv0", [128, TT], F32) as pv0, \
                 nc.psum_tensor("pv1", [128, TT], F32) as pv1:
                pvs = [pv0, pv1]
                B_at = (xc_at, xsq_at, re_at, rm_at, rv_at, sm_at[32:33, :],
                        zt_at, sm_at[0:1, :], sr_at, st0_at, st1_at,
                        bcm_at, bcr_at)

                def atv_phase(E):
                    t_ms = E.op("gpsimd", lambda g: g.memset(vtm[:], 1.0), inc=True)
                    E.wait("sync", "g", wprep[("v", 0)])
                    t_tr = None
                    for dc in range(NDC):
                        t_tr = E.tdma("s", out=wvT[:, TT * dc:TT * dc + TT],
                                      in_=wv_s[:, 128 * dc:128 * dc + 128])
                    ev_t = {}

                    def after_tt(tt, tz):
                        for tc in range(4 * tt, 4 * tt + 4):
                            fp = tc % 2
                            E.wait("tensor", "s", t_tr)
                            E.wait("tensor", "vector", ev_t.get(tc - 2))
                            E.wait("tensor", "vector", tz)
                            t_mm = None
                            for dc in range(NDC):
                                st, sp = (dc == 0), (dc == NDC - 1)
                                t_mm = E.op("tensor", lambda t, dc=dc, tc=tc, st=st, sp=sp, fp=fp: t.matmul(
                                    pvs[fp][:], z2[:, S * dc + 128 * tc: S * dc + 128 * tc + 128],
                                    wvT[:, TT * dc:TT * dc + TT], start=st, stop=sp), inc=sp)
                            E.wait("vector", "tensor", t_mm)
                            E.wait("vector", "gpsimd", t_ms)
                            ev_t[tc] = E.op("vector", lambda v, tc=tc, fp=fp: v.tensor_copy(
                                out=vtm[:, 520 * tc:520 * tc + 520].rearrange("p (g c) -> p g c", g=8)[:, :, 0:64],
                                in_=pvs[fp][:].rearrange("p (g c) -> p g c", g=8)), inc=True)

                    emit_prologue(E, x1f, z2[:], B_at, nbuf=1, after_tt=after_tt,
                                  sq1=True)

                base[0] = run_phase(nc, sems, base[0], atv_phase)

            with nc.sbuf_tensor("c2s", [64, S], F32) as c2, \
                 nc.sbuf_tensor("s2s", [64, S], F32) as s2, \
                 nc.sbuf_tensor("wTq", [128, 2 * 16 * 256], BF16) as wT, \
                 nc.sbuf_tensor("ab4", [64, 4 * TT], F32) as ab4, \
                 nc.psum_tensor("pq0", [128, TT], F32) as pq0, \
                 nc.psum_tensor("pq1", [128, TT], F32) as pq1:
                pq = [pq0, pq1]

                def qkv_phase(E):
                    E.dma("s", out=c2[:], in_=c2t)
                    t_cs = E.dma("s", out=s2[:], in_=s2t)

                    last_mm_of = {}
                    carry = {"rope": None, "mm": None}

                    def proj_rope(kind, nslab, src_s, dst):
                        last_rope = {}
                        tread_of = {}
                        if carry["rope"] is not None:
                            for k2 in [(-1, 2), (-1, 3), (0, -2), (0, -1), (1, -1)]:
                                last_rope[k2] = carry["rope"]
                            last_mm_of[(kind, -2)] = carry["mm_t"]
                            last_mm_of[(kind, -1)] = carry["mm_t"]

                        def emit_tr(sg):
                            par = sg % 2
                            tl = None
                            for dc in range(NDC):
                                E.wait("sync", "g", wprep[(kind, 2 * sg + 1)])
                                E.wait("sync", "tensor", last_mm_of.get((kind, sg - 2)))
                                tl = E.tdma(
                                    "s", out=wT[:, (par * 16 + dc) * 256:(par * 16 + dc) * 256 + 256],
                                    in_=src_s[256 * sg:256 * sg + 256, 128 * dc:128 * dc + 128])
                            tread_of[sg] = tl

                        for sg in range(nslab // 2):
                            par = sg % 2
                            if sg == 0:
                                emit_tr(0)
                            if sg + 1 < nslab // 2:
                                emit_tr(sg + 1)
                            tread_last = tread_of[sg]
                            for sl in range(2):
                                s_abs = 2 * sg + sl
                                for tt in range(NTT):
                                    fp = tt % 2
                                    E.wait("tensor", "s", tread_last)
                                    prevk = (s_abs, tt - 2) if tt >= 2 else (s_abs - 1, tt + 2)
                                    E.wait("tensor", "vector", last_rope.get(prevk))
                                    if s_abs == 0 and tt < 2:
                                        E.wait("tensor", "vector", last_rope.get((-1, 2)))
                                    t_mm = None
                                    for dc in range(NDC):
                                        st, sp = (dc == 0), (dc == NDC - 1)
                                        col = (par * 16 + dc) * 256 + 128 * sl
                                        t_mm = E.op("tensor", lambda t, col=col, st=st, sp=sp, dc=dc, tt=tt, fp=fp: t.matmul(
                                            pq[fp][:], wT[:, col:col + 128],
                                            z2[:, S * dc + TT * tt: S * dc + TT * tt + TT],
                                            start=st, stop=sp), inc=sp)
                                    if tt == NTT - 1 and sl == 1:
                                        last_mm_of[(kind, sg)] = t_mm
                                    E.wait("vector", "tensor", t_mm)
                                    cs = TT * tt
                                    E.op("vector", lambda v, fp=fp, cs=cs: v.tensor_mul(
                                        ab4[:, 0:TT], pq[fp][0:64, :], c2[:, cs:cs + TT]), inc=True)
                                    E.op("vector", lambda v, fp=fp, cs=cs: v.tensor_mul(
                                        ab4[:, TT:2 * TT], pq[fp][64:128, :], s2[:, cs:cs + TT]), inc=True)
                                    E.op("vector", lambda v, fp=fp, cs=cs: v.tensor_mul(
                                        ab4[:, 2 * TT:3 * TT], pq[fp][0:64, :], s2[:, cs:cs + TT]), inc=True)
                                    t4 = E.op("vector", lambda v, fp=fp, cs=cs: v.tensor_mul(
                                        ab4[:, 3 * TT:4 * TT], pq[fp][64:128, :], c2[:, cs:cs + TT]), inc=True)
                                    E.wait("vector", "vector", t4)
                                    dcol = S * s_abs + TT * tt
                                    E.op("vector", lambda v, dcol=dcol: v.tensor_sub(
                                        dst[0:32, dcol:dcol + TT], ab4[0:32, 0:TT], ab4[0:32, TT:2 * TT]), inc=True)
                                    E.op("vector", lambda v, dcol=dcol: v.tensor_sub(
                                        dst[64:96, dcol:dcol + TT], ab4[32:64, 0:TT], ab4[32:64, TT:2 * TT]), inc=True)
                                    E.op("vector", lambda v, dcol=dcol: v.tensor_add(
                                        dst[32:64, dcol:dcol + TT], ab4[0:32, 2 * TT:3 * TT], ab4[0:32, 3 * TT:4 * TT]), inc=True)
                                    t_rp = E.op("vector", lambda v, dcol=dcol: v.tensor_add(
                                        dst[96:128, dcol:dcol + TT], ab4[32:64, 2 * TT:3 * TT], ab4[32:64, 3 * TT:4 * TT]), inc=True)
                                    last_rope[(s_abs, tt)] = t_rp
                        carry["rope"] = last_rope[(nslab - 1, NTT - 1)]
                        carry["mm_t"] = last_mm_of[(kind, nslab // 2 - 1)]

                    E.wait("vector", "s", t_cs)
                    proj_rope("q", 16, wq_s, qf[:])
                    proj_rope("k", 4, wk_s, kf[:])
                    E.wait("sync", "s", E.n["s"])
                    E.wait("gpsimd", "g", E.n["g"])

                if max_phase >= 5:
                    base[0] = run_phase(nc, sems, base[0], qkv_phase)


        with nc.sbuf_tensor("woT", [128, 16 * S], BF16) as woT, \
             nc.sbuf_tensor("attn", [128, 16 * TT], BF16) as attnb, \
             nc.sbuf_tensor("Pt2", [128, 2 * 1024], BF16) as Pt2, \
             nc.sbuf_tensor("sAB", [65, 4 * TT], F32) as sAB, \
             nc.sbuf_tensor("recipr", [1, 2 * TT], F32) as recipr, \
             nc.sbuf_tensor("bcs", [64, 2 * TT], F32) as bcs, \
             nc.sbuf_tensor("x1t", [128, 2 * TT], F32) as x1t, \
             nc.sbuf_tensor("x2t", [128, 2 * TT], F32) as x2t, \
             nc.psum_tensor("psS0", [128, 1024], F32) as psS0, \
             nc.psum_tensor("psS1", [128, 1024], F32) as psS1, \
             nc.psum_tensor("osA", [65, TT], F32) as osA, \
             nc.psum_tensor("osB", [65, TT], F32) as osB, \
             nc.psum_tensor("pbca", [64, TT], F32) as ps_bc:
            psS = [psS0, psS1]

            def core_phase(E):
                t_wo = None
                E.wait("sync", "g", wprep[("o", 0)])
                for dc in range(NDC):
                    t_wo = E.tdma("s", out=woT[:, S * dc:S * dc + S],
                                  in_=wo_s[:, 128 * dc:128 * dc + 128])
                exp_d = {}
                pv_d = {}
                at_wr = {}
                os_free = {}
                normq = []
                st_norm = {"prev_cpB": None, "prev_bcB": None}
                prev_oproj = None
                x2_wr = {}

                def emit_norm(task):
                    j_, p_, spar, hA_, hB_ = task
                    base = 2 * TT * spar
                    E.wait("vector", "scalar", os_free[(j_, p_)])
                    E.wait("vector", "tensor", st_norm["prev_bcB"])
                    E.op("vector", lambda v: v.reciprocal(
                        out=recipr[:, 0:TT], in_=sAB[64:65, base:base + TT]), inc=True)
                    t_rc = E.op("vector", lambda v: v.reciprocal(
                        out=recipr[:, TT:2 * TT], in_=sAB[64:65, base + TT:base + 2 * TT]), inc=True)
                    E.wait("tensor", "vector", t_rc)
                    E.wait("tensor", "scalar", st_norm["prev_cpB"])
                    t_bcA = E.op("tensor", lambda t: t.matmul(
                        ps_bc[:], onesf[0:1, 0:64], recipr[:, 0:TT], start=True, stop=True), inc=True)
                    E.wait("scalar", "tensor", t_bcA)
                    t_cpA = E.op("scalar", lambda sc: sc.copy(out=bcs[:, 0:TT], in_=ps_bc[:]), inc=True)
                    E.wait("tensor", "scalar", t_cpA)
                    t_bcB = E.op("tensor", lambda t: t.matmul(
                        ps_bc[:], onesf[0:1, 0:64], recipr[:, TT:2 * TT], start=True, stop=True), inc=True)
                    st_norm["prev_bcB"] = t_bcB
                    E.wait("scalar", "tensor", t_bcB)
                    t_cpB = E.op("scalar", lambda sc: sc.copy(out=bcs[:, TT:2 * TT], in_=ps_bc[:]), inc=True)
                    st_norm["prev_cpB"] = t_cpB
                    E.wait("vector", "scalar", t_cpB)
                    chA, baA = hA_ // 2, 64 * (hA_ % 2)
                    chB, baB = hB_ // 2, 64 * (hB_ % 2)
                    E.op("vector", lambda v, chA=chA, baA=baA, base=base: v.tensor_mul(
                        attnb[baA:baA + 64, TT * chA:TT * chA + TT],
                        sAB[0:64, base:base + TT], bcs[:, 0:TT]), inc=True)
                    t_at = E.op("vector", lambda v, chB=chB, baB=baB, base=base: v.tensor_mul(
                        attnb[baB:baB + 64, TT * chB:TT * chB + TT],
                        sAB[0:64, base + TT:base + 2 * TT], bcs[:, TT:2 * TT]), inc=True)
                    at_wr[(j_, p_)] = t_at
                for j in range(NTT):
                    for p in range(16):
                        j2, tq_ = p // 4, p % 4
                        gA, gB = 2 * j2, 2 * j2 + 1
                        hA, hB = 8 * j2 + tq_, 8 * j2 + tq_ + 4
                        nch = 4 * j + 4
                        mask_d = {}

                        def emit_pv(cc):
                            par = cc % 2
                            qlo = max(0, 128 * (cc - 4 * j))
                            ncol = TT - qlo
                            E.wait("tensor", "scalar", exp_d[(j, p, cc)])
                            E.wait("tensor", "vector", mask_d.get((j, p, cc)))
                            if cc == 0:
                                E.wait("tensor", "scalar", os_free.get((j, p - 1), os_free.get((j - 1, 15))))
                            E.op("tensor", lambda t, cc=cc, gA=gA, qlo=qlo, ncol=ncol, j=j, par=par: t.matmul(
                                osA[:, qlo:qlo + ncol], vtm[:, 520 * cc + 65 * gA:520 * cc + 65 * gA + 65],
                                Pt2[:, 1024 * par + qlo:1024 * par + qlo + ncol],
                                start=(cc == 0), stop=(cc == 4 * j + 3)))
                            t_pv = E.op("tensor", lambda t, cc=cc, gB=gB, qlo=qlo, ncol=ncol, j=j, par=par: t.matmul(
                                osB[:, qlo:qlo + ncol], vtm[:, 520 * cc + 65 * gB:520 * cc + 65 * gB + 65],
                                Pt2[:, 1024 * par + 512 + qlo:1024 * par + 512 + qlo + ncol],
                                start=(cc == 0), stop=(cc == 4 * j + 3)), inc=True)
                            pv_d[(j, p, cc)] = t_pv

                        for c in range(nch):
                            fp = c % 2
                            qlo = max(0, 128 * (c - 4 * j))
                            ncol = TT - qlo
                            E.wait("tensor", "scalar", exp_d.get((j, p, c - 2)))
                            if c < 2:
                                E.wait("tensor", "vector", prev_oproj)
                            E.op("tensor", lambda t, c=c, j2=j2, p=p, j=j, qlo=qlo, ncol=ncol, fp=fp: t.matmul(
                                psS[fp][:, qlo:qlo + ncol],
                                kf[0:64, S * j2 + 128 * c:S * j2 + 128 * c + 128],
                                qf[0:64, S * p + TT * j + qlo:S * p + TT * j + qlo + ncol],
                                start=True, stop=True, tile_position=(0, 0)), inc=True)
                            t_s2 = E.op("tensor", lambda t, c=c, j2=j2, p=p, j=j, qlo=qlo, ncol=ncol, fp=fp: t.matmul(
                                psS[fp][:, 512 + qlo:512 + qlo + ncol],
                                kf[64:128, S * j2 + 128 * c:S * j2 + 128 * c + 128],
                                qf[64:128, S * p + TT * j + qlo:S * p + TT * j + qlo + ncol],
                                start=True, stop=True, tile_position=(64, 0)), inc=True)
                            if c >= 1:
                                emit_pv(c - 1)
                            E.wait("scalar", "tensor", t_s2)
                            E.wait("scalar", "tensor", pv_d.get((j, p, c - 2)))
                            t_eB = E.op("scalar", lambda sc, qlo=qlo, fp=fp, c=c: sc.activation(
                                out=Pt2[:, 1024 * (c % 2) + qlo:1024 * (c % 2) + 1024],
                                in_=psS[fp][:, qlo:1024],
                                func=AF.Exp, scale=0.125), inc=True)
                            exp_d[(j, p, c)] = t_eB
                            if c >= 4 * j:
                                E.wait("vector", "scalar", t_eB)
                                E.op("vector", lambda v, qlo=qlo, c=c: v.tensor_mul(
                                    Pt2[:, 1024 * (c % 2) + qlo:1024 * (c % 2) + qlo + 128],
                                    Pt2[:, 1024 * (c % 2) + qlo:1024 * (c % 2) + qlo + 128], tri[:]), inc=True)
                                mask_d[(j, p, c)] = E.op("vector", lambda v, qlo=qlo, c=c: v.tensor_mul(
                                    Pt2[:, 1024 * (c % 2) + 512 + qlo:1024 * (c % 2) + 512 + qlo + 128],
                                    Pt2[:, 1024 * (c % 2) + 512 + qlo:1024 * (c % 2) + 512 + qlo + 128],
                                    tri[:]), inc=True)
                        emit_pv(nch - 1)
                        t_pvl = pv_d[(j, p, nch - 1)]
                        spar = p % 2
                        E.wait("scalar", "tensor", t_pvl)
                        E.wait("scalar", "vector", at_wr.get((j, p - 2), at_wr.get((j - 1, 14 + p))))
                        E.op("scalar", lambda sc, spar=spar: sc.copy(
                            out=sAB[:, 2 * TT * spar:2 * TT * spar + TT], in_=osA[:]))
                        t_ev = E.op("scalar", lambda sc, spar=spar: sc.copy(
                            out=sAB[:, 2 * TT * spar + TT:2 * TT * spar + 2 * TT], in_=osB[:]), inc=True)
                        os_free[(j, p)] = t_ev
                        normq.append((j, p, spar, hA, hB))
                        if len(normq) > 1:
                            emit_norm(normq.pop(0))
                    while normq:
                        emit_norm(normq.pop(0))
                    E.wait("tensor", "s", t_wo)
                    oproj_ev = {}
                    for d16 in range(NDC):
                        fp2 = d16 % 2
                        E.wait("tensor", "vector", at_wr[(j, 15)])
                        E.wait("tensor", "vector", oproj_ev.get(d16 - 2))
                        t_mo = None
                        for dc in range(NDC):
                            st, sp = (dc == 0), (dc == NDC - 1)
                            t_mo = E.op("tensor", lambda t, dc=dc, d16=d16, st=st, sp=sp, fp2=fp2: t.matmul(
                                psS[fp2][:, 0:TT], woT[:, S * dc + 128 * d16:S * dc + 128 * d16 + 128],
                                attnb[:, TT * dc:TT * dc + TT], start=st, stop=sp), inc=sp)
                        E.wait("sync", "vector", oproj_ev.get(d16 - 2))
                        t_xl = E.dma("s", out=x1t[:, TT * fp2:TT * fp2 + TT], in_=x1f[d16, j])
                        E.wait("vector", "tensor", t_mo)
                        E.wait("vector", "s", t_xl)
                        E.wait("vector", "s", x2_wr.get(fp2))
                        t_x2 = E.op("vector", lambda v, fp2=fp2: v.tensor_add(
                            x2t[:, TT * fp2:TT * fp2 + TT], psS[fp2][:, 0:TT],
                            x1t[:, TT * fp2:TT * fp2 + TT]), inc=True)
                        oproj_ev[d16] = t_x2
                        E.wait("sync", "vector", t_x2)
                        x2_wr[fp2] = E.dma("s", out=x2f[d16, j], in_=x2t[:, TT * fp2:TT * fp2 + TT])
                    prev_oproj = oproj_ev[NDC - 1]
                E.wait("sync", "s", E.n["s"])

            if max_phase >= 7:
                base[0] = run_phase(nc, sems, base[0], core_phase)

    if max_phase >= 8:
        with ExitStack() as f2:
            fsb = lambda n, sh, dt: f2.enter_context(nc.sbuf_tensor(n, sh, dt))
            fps = lambda n: f2.enter_context(nc.psum_tensor(n, [128, TT], F32))
            z3 = fsb("z_f2", [128, NDC * S], BF16)
            xc2 = fsb("xcf2", [128, 2 * NDC * TT], BF16)
            xsq2 = fsb("xqf2", [128, 2 * TT], BF16)
            re2 = fsb("ref2", [1, TT], F32)
            rm2 = fsb("rmf2", [1, TT], F32)
            rv2 = fsb("rvf2", [1, TT], F32)
            rl2 = fsb("rlf2", [1, TT], F32)
            zt2 = fsb("ztf2", [128, TT], F32)
            sm2 = fsb("smf2", [1, TT], F32)
            sr2 = fsb("srf2", [1, TT], F32)
            wgT2 = fsb("wgTf2", [128, 2 * 16 * TT], BF16)
            wuT2 = fsb("wuTf2", [128, 2 * 16 * TT], BF16)
            htile2 = fsb("htf2", [128, 2 * S], BF16)
            sgb2 = fsb("sgf2", [128, 2 * TT], F32)
            pgF = [fps("pgF0"), fps("pgF1"), fps("pgF2"), fps("pgF3")]
            puF = [fps("puF0"), fps("puF1"), fps("puF2"), fps("puF3")]
            B2 = (xc2, xsq2, re2, rm2, rv2, rl2, zt2, sm2, sr2,
                  puF[0][0:1, :], puF[1][0:1, :], pgF[2], pgF[3])

            def blockF2(E):
                zt3t = emit_prologue(E, x2f, z3[:], B2, nbuf=2)
                emit_gu(E, z3[:], wgT2, wuT2, htile2, sgb2, pgF, puF,
                        z_ticket=zt3t)
                E.wait("gpsimd", "g", E.n["g"])
                E.wait("sync", "s", E.n["s"])

            base[0] = run_phase(nc, sems, base[0], blockF2)
        if max_phase >= 10:
            ffn_proj(x2f, None, final=True, tag="f2")

    gctx.close()
    return nc


_NC_CACHE = None


def _host_constants():
    t = np.arange(S, dtype=np.float32)
    inv_freq = (1.0 / np.power(np.float32(10000.0),
                               np.arange(0, 64, 2, dtype=np.float32) / np.float32(64.0))).astype(np.float32)
    freqs = np.outer(t, inv_freq).astype(np.float32)
    cos = np.cos(freqs).astype(np.float32).T
    sin = np.sin(freqs).astype(np.float32).T
    c2 = np.ascontiguousarray(np.concatenate([cos, cos], axis=0))
    s2 = np.ascontiguousarray(np.concatenate([sin, sin], axis=0))
    kidx = np.arange(128)[:, None]
    qidx = np.arange(128)[None, :]
    tri = (qidx >= kidx).astype(bfd)
    onesf = np.ones((1, 128), np.float32)
    halvesf = np.full((1, 128), 0.5, np.float32)
    onesbf = np.ones((128, 1), bfd)
    idf = np.eye(128, dtype=np.float32)
    return c2, s2, tri, onesf, halvesf, onesbf, idf


def kernel(L0_attn_x, L0_attn_wq, L0_attn_wk, L0_attn_wv, L0_attn_wo,
           L0_ffn_w_gate_up, L0_ffn_w_proj):
    global _NC_CACHE
    if _NC_CACHE is None:
        _NC_CACHE = build_program()
    nc = _NC_CACHE
    c2, s2, tri, onesf, halvesf, onesbf, idf = _host_constants()
    basem = {
        "wq": np.ascontiguousarray(np.asarray(L0_attn_wq), np.float32),
        "wk": np.ascontiguousarray(np.asarray(L0_attn_wk), np.float32),
        "wv": np.ascontiguousarray(np.asarray(L0_attn_wv), np.float32),
        "wo": np.ascontiguousarray(np.asarray(L0_attn_wo), np.float32),
        "wgu": np.ascontiguousarray(np.asarray(L0_ffn_w_gate_up), np.float32),
        "wp": np.ascontiguousarray(np.asarray(L0_ffn_w_proj), np.float32),
        "c2t": c2, "s2t": s2, "tri": tri, "onesf": onesf,
        "halvesf": halvesf, "onesbf": onesbf, "idf": idf,
        "epsc": np.array([[EPS_LN, EPS_RMS]], np.float32),
    }
    xfull = np.asarray(L0_attn_x)
    in_maps = []
    for b in range(8):
        m = dict(basem)
        m["x"] = np.ascontiguousarray(xfull[b], np.float32)
        in_maps.append(m)
    res = run_bass_kernel_spmd(nc, in_maps, core_ids=list(range(8)))
    return np.stack([np.asarray(r["out"], np.float32) for r in res.results], axis=0)
